# version 29
# baseline (speedup 1.0000x reference)
import numpy as np
import ml_dtypes
from contextlib import ExitStack

import concourse.bass as bass
import concourse.mybir as mybir
from concourse.bass_utils import run_bass_kernel_spmd

F32 = mybir.dt.float32
BF16 = mybir.dt.bfloat16
AF = mybir.ActivationFunctionType
ALU = mybir.AluOpType
AX = mybir.AxisListType
BF16NP = ml_dtypes.bfloat16


def real_cfg():
    return dict(T=4096, D=4096, H=32, KVH=8, NQ=2048, GT=2048, TG=1024)


def half_blocks(T, half):
    nw = T // 1024
    if half == 0:
        return [8 * j + d for j in range(nw) for d in (0, 3, 4, 7)]
    return [8 * j + d for j in range(nw) for d in (1, 2, 5, 6)]


def build_masks(cfg, half):
    T, NQ = cfg["T"], cfg["NQ"]
    NQT = NQ // 512
    blocks = half_blocks(T, half)
    m = np.zeros((NQT, 8, 128, 128), np.float32)
    for j in range(NQT):
        gq = np.concatenate([np.arange(b * 128, (b + 1) * 128) for b in blocks[4 * j:4 * j + 4]])
        for c in range(8):
            kk = (8 * j + c) * 128 + np.arange(128)
            qq = gq[(c // 2) * 128:(c // 2) * 128 + 128]
            m[j, c] = (kk[:, None] <= qq[None, :]).astype(np.float32)
    return m.astype(BF16NP)



class IR:

    def __init__(self):
        self.ops = []
        self.phases = []
        self._cur = None
        self._last = {}
        self._barrier = []

    def phase(self, name):
        self._barrier = sorted(self._last.values())
        self._cur = []
        self.phases.append((name, self._cur))

    def op(self, eng, fn, deps=(), dma=False, chan=None):
        i = len(self.ops)
        assert (chan is not None) == dma, "dma ops need a chan; compute ops must not"
        self.ops.append(dict(eng=eng, fn=fn, dma=dma, chan=chan,
                             deps=sorted({d for d in deps if d is not None} | set(self._barrier)),
                             sig=False, sem=None, val=None))
        self._cur.append(i)
        self._last[chan if dma else eng] = i
        return i

    def resolve(self, nc, ctx):
        for o in self.ops:
            for d in o["deps"]:
                self.ops[d]["sig"] = True
        sems, counts = {}, {}
        for pi, (pname, opids) in enumerate(self.phases):
            for i in opids:
                o = self.ops[i]
                if not (o["sig"] or o["dma"]):
                    continue
                if o["dma"]:
                    key = ("d", o["chan"])
                else:
                    key = ("c", o["eng"], pi)
                if key not in sems:
                    sems[key] = ctx.enter_context(nc.semaphore("_".join(str(k) for k in key)))
                    counts[key] = 0
                counts[key] += 16 if o["dma"] else 1
                o["sem"], o["val"] = sems[key], counts[key]

    def emit(self, nc):
        engmap = dict(sp="sync", act="scalar", dve="vector", pe="tensor", pool="gpsimd")
        for pi, (pname, opids) in enumerate(self.phases):
            byeng = {}
            for i in opids:
                byeng.setdefault(self.ops[i]["eng"], []).append(i)
            with nc.Block(pname) as block:
                for eng, ids in byeng.items():
                    def body(e, ids=ids):
                        waited = {}
                        for i in ids:
                            o = self.ops[i]
                            for d in o["deps"]:
                                p = self.ops[d]
                                if waited.get(p["sem"].num, -1) < p["val"]:
                                    e.wait_ge(p["sem"], p["val"])
                                    waited[p["sem"].num] = p["val"]
                            if o["fn"] is not None:
                                inst = o["fn"](e)
                                if o["sig"] or o["dma"]:
                                    inst.then_inc(o["sem"], 16 if o["dma"] else 1)
                    getattr(block, engmap[eng])(body)



def build_program(cfg):
    T, D, H, KVH = cfg["T"], cfg["D"], cfg["H"], cfg["KVH"]
    NQ, GT, TG = cfg["NQ"], cfg["GT"], cfg["TG"]
    DC = D // 128
    DKV = KVH * 128
    REP = H // KVH
    NG = T // GT
    NSUB = GT // 512
    NSUBQ = NQ // 512
    NQT = NQ // 512
    NTG = NQ // TG
    TSG = TG // 128
    NOC = D // 512
    NKC = T // 128
    ISQ = 1.0 / float(np.sqrt(128.0))
    EPS = 1e-5
    SCALE = 2.0

    nc = bass.Bass("TRN2", target_bir_lowering=False, debug=False)

    def din(name, shape, dt=F32):
        return nc.dram_tensor(name, shape, dt, kind="ExternalInput").ap()

    xt_kv = din("xt_kv", [D, T]); xt_q = din("xt_q", [D, NQ]); x_q = din("x_q", [NQ, D])
    wq = din("wq", [D, D]); wk = din("wk", [D, DKV]); wv = din("wv", [D, DKV])
    wo = din("wo", [D, D]); wff = din("wff", [D, D])
    at = din("at", [D, 128]); bt = din("bt", [128, D])
    bq = din("bq", [D]); bk = din("bk", [DKV]); bv = din("bv", [DKV])
    bo = din("bo", [D]); bff = din("bff", [D]); lw = din("lw", [128])
    masks = din("masks", [NQT, 8, 128, 128], BF16)
    ident = din("ident", [128, 128], BF16)
    onescol = din("onescol", [128, 1], BF16)
    onesrowf = din("onesrowf", [1, 128])
    onesrowb = din("onesrowb", [1, 128], BF16)
    y = nc.dram_tensor("y", [NQ, D], F32, kind="ExternalOutput").ap()

    k_dram = nc.dram_tensor("k_scr", [KVH, 128, T], BF16, kind="Internal").ap()
    v_dram = nc.dram_tensor("v_scr", [KVH, 128, T], BF16, kind="Internal").ap()
    q_dram = nc.dram_tensor("q_scr", [H, 128, NQ], BF16, kind="Internal").ap()
    a_dram = nc.dram_tensor("a_scr", [NQT, 128, H * 512], BF16, kind="Internal").ap()
    r1_dram = nc.dram_tensor("r1_scr", [NQ, D], F32, kind="Internal").ap()
    r2_dram = nc.dram_tensor("r2_scr", [NQ, D], F32, kind="Internal").ap()

    ir = IR()
    stack = ExitStack()
    guards = []

    def sb(name, shape, dt):
        g = nc.sbuf_tensor(name, shape, dt)
        return g.__enter__(), g

    def ps(name, shape, dt=F32):
        g = nc.psum_tensor(name, shape, dt)
        return g.__enter__(), g

    def freeall(gs):
        for g in reversed(gs):
            g.__exit__(None, None, None)

    def sbp(name, shape, dt):
        t, g = sb(name, shape, dt)
        guards.append(g)
        return t

    bk_sb = sbp("bk_sb", [128, KVH], F32)
    bv_sb = sbp("bv_sb", [128, KVH], F32)
    bq_sb = sbp("bq_sb", [128, H], F32)
    lw_sb = sbp("lw_sb", [128, 1], F32)
    coef = sbp("coef", [128, 1], F32)
    eps_sb = sbp("eps_sb", [128, 1], F32)
    onep5_sb = sbp("onep5_sb", [128, 1], F32)
    id_sb = sbp("id_sb", [128, 128], BF16)
    ocol_sb = sbp("ocol_sb", [128, 1], BF16)
    orf_sb = sbp("orf_sb", [1, 128], F32)
    orb_sb = sbp("orb_sb", [1, 128], BF16)

    ir.phase("setup")
    NCslow = dict(allow_slow_non_contiguous=True)
    d_bk = ir.op("sp", lambda e: e.dma_start(out=bk_sb[:], in_=bk.rearrange("(i p) -> p i", p=128), **NCslow), dma=True, chan="st")
    d_bv = ir.op("sp", lambda e: e.dma_start(out=bv_sb[:], in_=bv.rearrange("(i p) -> p i", p=128), **NCslow), deps=[d_bk], dma=True, chan="st")
    d_bq = ir.op("sp", lambda e: e.dma_start(out=bq_sb[:], in_=bq.rearrange("(i p) -> p i", p=128), **NCslow), deps=[d_bv], dma=True, chan="st")
    d_lw = ir.op("sp", lambda e: e.dma_start(out=lw_sb[:], in_=lw.rearrange("(p o) -> p o", o=1), **NCslow), deps=[d_bq], dma=True, chan="st")
    d_id = ir.op("sp", lambda e: e.dma_start(out=id_sb[:], in_=ident[:]), deps=[d_lw], dma=True, chan="st")
    d_oc = ir.op("sp", lambda e: e.dma_start(out=ocol_sb[:], in_=onescol[:]), deps=[d_id], dma=True, chan="st")
    d_orf = ir.op("sp", lambda e: e.dma_start(out=orf_sb[:], in_=onesrowf[:]), deps=[d_oc], dma=True, chan="st")
    d_orb = ir.op("sp", lambda e: e.dma_start(out=orb_sb[:], in_=onesrowb[:]), deps=[d_orf], dma=True, chan="st")
    biasdeps = [d_bq]
    op_coef = ir.op("act", lambda e: e.activation(out=coef[:], in_=lw_sb[:], func=AF.Identity, scale=SCALE), deps=[d_lw])
    op_eps = ir.op("dve", lambda e: e.memset(eps_sb[:], EPS))
    op_1p5 = ir.op("dve", lambda e: e.memset(onep5_sb[:], 1.5))

    ir.phase("qkv")
    ga = []
    GX = max(GT, NQ)
    x_bf, g = sb("x_bf", [128, DC * GX], BF16); ga.append(g)
    w_bf, g = sb("w_bf", [128, DC * 512], BF16); ga.append(g)
    xstage = []
    for i in range(2):
        t, g = sb(f"xstage{i}", [128, GX], F32); ga.append(g); xstage.append(t)
    estage = []
    for i in range(4):
        t, g = sb(f"estage{i}", [128, 512], BF16); ga.append(g); estage.append(t)
    pa = []
    pag = []
    for i in range(8):
        t, g = ps(f"pa{i}", [128, 512]); pag.append(g); pa.append(t)

    kv_chunks = ([("k", wk, i, bk_sb, i, k_dram) for i in range(KVH)] +
                 [("v", wv, i, bv_sb, i, v_dram) for i in range(KVH)])
    q_chunks = [("q", wq, i, bq_sb, i, q_dram) for i in range(H)]

    last_wr = {"k": None, "v": None, "q": None}
    bank_prev = {}
    est_wr = {}
    ws_cast = {}
    wpe_last = {}
    xld_prev = [None, None, None, None]
    wld_prev = [None, None]
    wa_use = {}
    w_loaded = [None]
    w_saved = [None]
    pe_last_half = {0: None, 1: None}
    pe_last_half_cur = {0: None, 1: None}
    ev_i, ws_i = [0], [0]
    last_pe_op = [None]

    def pe_op(fn, deps):
        i = ir.op("pe", fn, deps)
        last_pe_op[0] = i
        return i

    x_bf3 = x_bf[:].rearrange("p (c t) -> p c t", c=DC)
    w_bf3 = w_bf[:].rearrange("p (c n) -> p c n", c=DC)

    def gen_gemm(src, gcount, gt, nsub, chunks, oc_base, same_layout=False):
        halves = 2 if nsub >= 2 else 1
        gth = gt // halves
        nsh = nsub // halves
        x3loc = x_bf[:].rearrange("p (c t) -> p c t", c=DC) if gt == GX else \
            x_bf[:, 0:DC * gt].rearrange("p (c t) -> p c t", c=DC)
        for gi in range(gcount):
            xc_ops = {}
            for half in range(halves):
                hdeps = [pe_last_half[half]] if (gi > 0 or same_layout) \
                    else [pe_last_half[0], pe_last_half[1]]
                for Dc2 in range(DC // 2):
                    Dc = 2 * Dc2
                    ld = ir.op("pool", lambda e, Dc=Dc, gi=gi, half=half, src=src, gt=gt, gth=gth, x3loc=x3loc: e.dma_start(
                        out=x3loc[:, Dc:Dc + 2, half * gth:(half + 1) * gth],
                        in_=src[Dc * 128:(Dc + 2) * 128,
                                gi * gt + half * gth:gi * gt + (half + 1) * gth].rearrange(
                                    "(c p) t -> p c t", p=128)),
                        deps=[xld_prev[Dc2 % 4]] + hdeps, dma=True, chan=f"xa{Dc2 % 4}")
                    xld_prev[Dc2 % 4] = ld
                    xc_ops[(half, Dc)] = ld
                    xc_ops[(half, Dc + 1)] = ld
            slabs = []
            for kind in ("k", "v", "q"):
                kc_ = [ch for ch in chunks if ch[0] == kind]
                slabs += [kc_[i:i + 4] for i in range(0, len(kc_), 4)]
            if gi % 2 == 1:
                slabs = slabs[::-1]
            for slab in slabs:
                width = len(slab) * 128
                wten, col0 = slab[0][1], slab[0][2] * 128
                skey = (slab[0][0], slab[0][2], width, id(wten))
                if w_loaded[0] == skey:
                    wc_ops = w_saved[0]
                else:
                    wc_ops = {}
                    for Dc4 in range(DC // 4):
                        dset = list(range(Dc4 * 4, Dc4 * 4 + 4))
                        ld = ir.op("pool", lambda e, wten=wten, Dc4=Dc4, col0=col0, width=width: e.dma_start(
                            out=w_bf3[:, Dc4 * 4:(Dc4 + 1) * 4, 0:width],
                            in_=wten[Dc4 * 512:(Dc4 + 1) * 512, col0:col0 + width].rearrange(
                                "(c p) n -> p c n", p=128)),
                            deps=[wld_prev[Dc4 % 2]] + [wa_use.get(d) for d in dset],
                            dma=True, chan=f"wa{Dc4 % 2}")
                        wld_prev[Dc4 % 2] = ld
                        for d in dset:
                            wc_ops[d] = ld
                new_lastuse = {}
                is_first_after = (slab is slabs[0]) and (gi > 0 or same_layout)
                if (slab is slabs[-1]) or is_first_after:
                    pairs = [(ci, half) for half in range(halves) for ci in range(len(slab))]
                else:
                    pairs = [(ci, half) for ci in range(len(slab)) for half in range(halves)]
                for ci, half in pairs:
                    kind, _w, chunk_i, bias_sb, bias_i, dest = slab[ci]
                    par = ci % 2
                    stop_ops = {}
                    for Dc in range(DC):
                        for sub in range(nsh):
                            bank = par * 4 + half * nsh + sub
                            deps = [wc_ops[Dc], xc_ops[(half, Dc)]]
                            if Dc == 0:
                                deps.append(bank_prev.get(bank))
                            mm = pe_op(lambda e, bank=bank, Dc=Dc, ci=ci, half=half, sub=sub, gt=gt, gth=gth: e.matmul(
                                pa[bank][:],
                                w_bf[:, Dc * 512 + ci * 128:Dc * 512 + (ci + 1) * 128],
                                x_bf[:, Dc * gt + half * gth + sub * 512:Dc * gt + half * gth + sub * 512 + 512],
                                start=(Dc == 0), stop=(Dc == DC - 1)), deps)
                            if Dc == DC - 1:
                                stop_ops[sub] = mm
                            pe_last_half_cur[half] = mm
                            new_lastuse[Dc] = mm
                    for sub in range(nsh):
                        bank = par * 4 + half * nsh + sub
                        i = ev_i[0]; slot = i % 4
                        act = ir.op("act", lambda e, bank=bank, slot=slot, bias_sb=bias_sb, bias_i=bias_i: e.activation(
                            out=estage[slot][:], in_=pa[bank][:], func=AF.Identity,
                            bias=bias_sb[:, bias_i:bias_i + 1]),
                            deps=[stop_ops[sub], est_wr.get(i - 4)] + biasdeps)
                        bank_prev[bank] = act
                        if kind == "q":
                            c0 = half * gth + sub * 512
                        else:
                            c0 = gi * gt + half * gth + sub * 512
                        wr = ir.op("act", lambda e, dest=dest, chunk_i=chunk_i, c0=c0, slot=slot: e.dma_start(
                            out=dest[chunk_i, :, c0:c0 + 512], in_=estage[slot][:]),
                            deps=[act], dma=True, chan=f"es{slot}")
                        est_wr[i] = wr
                        last_wr[kind] = wr
                        ev_i[0] += 1
                for d in range(DC):
                    wa_use[d] = new_lastuse[d]
                w_loaded[0] = skey
                w_saved[0] = wc_ops
            pe_last_half[0] = pe_last_half_cur[0]
            pe_last_half[1] = pe_last_half_cur[1]

    gen_gemm(xt_kv, NG, GT, NSUB, kv_chunks, 0)
    gen_gemm(xt_q, 1, NQ, NSUBQ, q_chunks, len(kv_chunks),
             same_layout=(NQ == GT))
    a_done = [last_wr["k"], last_wr["v"], last_wr["q"]]
    freeall(pag)
    freeall(ga)

    ir.phase("attn")
    gb = []
    k_sb, g = sb("k_sb", [128, KVH * T], BF16); gb.append(g)
    v_sb, g = sb("v_sb", [128, NKC * DKV], BF16); gb.append(g)
    q_sb = []
    for i in range(2):
        t, g = sb(f"q_sb{i}", [128, 512], BF16); gb.append(g); q_sb.append(t)
    pt_sb = []
    for i in range(4):
        t, g = sb(f"pt_sb{i}", [128, 512], BF16); gb.append(g); pt_sb.append(t)
    mask_sb, g = sb("mask_sb", [128, 8 * 128], BF16); gb.append(g)
    aslab, g = sb("aslab", [128, H * 512], BF16); gb.append(g)
    t, g = sb("rt_sb0", [128, 512], F32); gb.append(g)
    rt_sb = [t, t]
    t, g = sb("den_sb0", [1, 512], F32); gb.append(g)
    den_sb = [t, t]
    t, g = sb("rec_sb0", [1, 512], F32); gb.append(g)
    rec_sb = [t, t]

    kld = {}
    vtld = {}
    vt_prev = {}

    def need_k(kv):
        if kv not in kld:
            kld[kv] = ir.op("sp", lambda e, kv=kv: e.dma_start(
                out=k_sb[:, kv * T:(kv + 1) * T], in_=k_dram[kv]),
                deps=a_done + [kld.get(kv - 1)], dma=True, chan="kld")
        return kld[kv]

    def need_v(kv, kc):
        if (kv, kc) not in vtld:
            vtld[(kv, kc)] = ir.op("sp", lambda e, kv=kv, kc=kc: e.dma_start(
                out=v_sb[:, kc * DKV + kv * 128:kc * DKV + (kv + 1) * 128],
                in_=v_dram[kv, :, kc * 128:(kc + 1) * 128], transpose=True),
                deps=a_done + [vt_prev.get(kv)], dma=True, chan=f"vt{kv}")
            vt_prev[kv] = vtld[(kv, kc)]
        return vtld[(kv, kc)]

    psg = []
    def ps2(name):
        t, g = ps(name, [128, 512]); psg.append(g); return t
    ps_s = [ps2(f"ps_s{i}") for i in range(4)]
    ps_av = [ps2(f"ps_av{i}") for i in range(2)]
    t, g = ps("ps_den0", [1, 512]); psg.append(g)
    ps_den = [t, t]
    ps_rt = [ps2("ps_rt0")] * 2

    ptlast = {}
    sfree = {}
    qlast = {}
    at_evac = {}
    rt_evac = {}
    dclast = {}
    lnlast = {}
    rtmm_last = {}
    av_par = {}
    tail_prev = [None]
    swq = {}
    slab_wr_q = {}
    QD = max(H // 4, 1)
    mask_last = [None]
    b_attn_done = None

    for qt in range(NQT):
        NK = 8 * (qt + 1)
        mld = ir.op("sp", lambda e, qt=qt: e.dma_start(
            out=mask_sb[:].rearrange("p (c q) -> p c q", c=8),
            in_=masks[qt].rearrange("c p q -> p c q")),
            deps=a_done + [mask_last[0]], dma=True, chan="mld")
        LAG = 3
        queue = []
        tails = []

        def pump():
            fn = queue.pop(0)
            fn()
            for tset in tails:
                tset[0] -= 1
            while tails and tails[0][0] <= 0:
                tails.pop(0)[1]()

        for h in range(H):
            kv = h // REP
            hp = h % 2
            ql = ir.op("sp", lambda e, h=h, qt=qt, hp=hp: e.dma_start(
                out=q_sb[hp][:], in_=q_dram[h, :, qt * 512:(qt + 1) * 512]),
                deps=[a_done[2], qlast.get(hp)], dma=True, chan=f"ql{hp}")
            exp_cur = {}
            stopv = {}

            def srange(kc, NK=NK):
                if kc >= NK - 8:
                    return 128 * ((kc - (NK - 8)) // 2)
                return 0

            def mk_avden(kc, h=h, kv=kv, hp=hp, NK=NK, exp_cur=exp_cur, stopv=stopv, srange=srange):
                def go():
                    s = srange(kc)
                    par = kc % 4
                    cur = exp_cur[kc]
                    avdeps = [cur, need_v(kv, kc)]
                    if kc == 0:
                        avdeps += [av_par.get(hp), dclast.get(0)]
                    av = pe_op(lambda e, kc=kc, kv=kv, par=par, hp=hp, NK=NK, s=s: e.matmul(
                        ps_av[hp][:, s:512], v_sb[:, kc * DKV + kv * 128:kc * DKV + (kv + 1) * 128],
                        pt_sb[par][:, s:512], start=(kc == 0), stop=(kc == NK - 1)), avdeps)
                    dn = pe_op(lambda e, kc=kc, par=par, hp=hp, NK=NK, s=s: e.matmul(
                        ps_den[hp][:, s:512], ocol_sb[:], pt_sb[par][:, s:512],
                        start=(kc == 0), stop=(kc == NK - 1)), [cur, d_oc])
                    ptlast[par] = dn
                    if kc == NK - 1:
                        stopv[0], stopv[1] = av, dn
                return go

            for kc in range(NK):
                par = kc % 4
                s = srange(kc)
                mm = pe_op(lambda e, kc=kc, kv=kv, par=par, hp=hp, s=s: e.matmul(
                    ps_s[par][:, s:512], k_sb[:, kv * T + kc * 128:kv * T + (kc + 1) * 128],
                    q_sb[hp][:, s:512], start=True, stop=True),
                    [ql, need_k(kv), sfree.get(par)])
                qlast[hp] = mm
                ex = ir.op("act", lambda e, par=par, s=s: e.activation(
                    out=pt_sb[par][:, s:512], in_=ps_s[par][:, s:512], func=AF.Exp, scale=ISQ),
                    deps=[mm, ptlast.get(par)])
                sfree[par] = ex
                cur = ex
                if kc >= NK - 8:
                    c = kc - (NK - 8)
                    cur = ir.op("dve", lambda e, par=par, c=c, s=s: e.tensor_tensor(
                        out=pt_sb[par][:, s:s + 128], in0=pt_sb[par][:, s:s + 128],
                        in1=mask_sb[:, c * 128:(c + 1) * 128], op=ALU.mult),
                        deps=[ex, mld])
                    mask_last[0] = cur
                exp_cur[kc] = cur
                queue.append(mk_avden(kc))
                if len(queue) > LAG:
                    pump()

            def make_tails(h=h, hp=hp, stopv=stopv):
                carry = {}

                def tail1():
                    stopDen = stopv[1]
                    dc = ir.op("dve", lambda e, hp=hp: e.tensor_copy(
                        out=den_sb[hp][:], in_=ps_den[hp][:]), deps=[stopDen, lnlast.get(0)])
                    dclast[0] = dc
                    ln1 = ir.op("act", lambda e, hp=hp: e.activation(
                        out=rec_sb[hp][:], in_=den_sb[hp][:], func=AF.Ln), deps=[dc, rtmm_last.get(0)])
                    dr = ir.op("act", lambda e, hp=hp: e.activation(
                        out=rec_sb[hp][:], in_=rec_sb[hp][:], func=AF.Exp, scale=-1.0),
                        deps=[ln1])
                    lnlast[0] = dr
                    carry["dr"] = dr

                def tail2(qt=qt):
                    stopAV = stopv[0]
                    rt = pe_op(lambda e, hp=hp: e.matmul(
                        ps_rt[hp][:], orf_sb[:], rec_sb[hp][:], start=True, stop=True),
                        [carry["dr"], d_orf, rt_evac.get(0)])
                    rtmm_last[0] = rt
                    re = ir.op("act", lambda e, hp=hp: e.activation(
                        out=rt_sb[hp][:], in_=ps_rt[hp][:], func=AF.Identity),
                        deps=[rt, at_evac.get(0)])
                    rt_evac[0] = re
                    qd = h // QD
                    ae = ir.op("dve", lambda e, h=h, hp=hp: e.tensor_tensor(
                        out=aslab[:, h * 512:(h + 1) * 512], in0=ps_av[hp][:],
                        in1=rt_sb[hp][:], op=ALU.mult), deps=[stopAV, re, swq.get(qd)])
                    av_par[hp] = ae
                    at_evac[0] = ae
                    if h % QD == QD - 1:
                        w = ir.op("sp", lambda e, qt=qt, qd=qd: e.dma_start(
                            out=a_dram[qt, :, qd * QD * 512:(qd + 1) * QD * 512],
                            in_=aslab[:, qd * QD * 512:(qd + 1) * QD * 512]),
                            deps=[ae, swq.get(qd)], dma=True, chan=f"sw{qd}")
                        swq[qd] = w
                        slab_wr_q.setdefault(qt, []).append(w)
                return tail1, tail2
            t1, t2 = make_tails()
            tails.append([LAG, t1])
            tails.append([LAG + 2, t2])
        while queue:
            pump()
        for tset in tails:
            tset[1]()
        tails.clear()
        b_attn_done = slab_wr_q[qt][-1]
    freeall(psg)
    freeall(gb)

    ir.phase("post")
    gc = []
    at_bf, g = sb("at_bf", [128, DC * 128], BF16); gc.append(g)
    bt_bf, g = sb("bt_bf", [128, D], BF16); gc.append(g)
    rowbuf = []
    for i in range(2):
        t, g = sb(f"rowbuf{i}", [128, D], F32); gc.append(g); rowbuf.append(t)
    outst = []
    for i in range(5):
        t, g = sb(f"outst{i}", [128, D // 4], F32); gc.append(g); outst.append(t)
    scr_bf, g = sb("scr_bf", [128, 512], BF16); gc.append(g)
    hT_sb, g = sb("hT_sb", [128, TG], BF16); gc.append(g)
    w_bf2 = []
    for i in range(2):
        t, g = sb(f"w_bf2{i}", [128, 4 * 512], BF16); gc.append(g); w_bf2.append(t)
    xres = []
    for i in range(8):
        t, g = sb(f"xres{i}", [128, 512], F32); gc.append(g); xres.append(t)
    rst = []
    for i in range(8):
        t, g = sb(f"rst{i}", [128, 512], F32); gc.append(g); rst.append(t)
    stat_mean, g = sb("stat_mean", [128, TSG], F32); gc.append(g)
    stat_rstd, g = sb("stat_rstd", [128, TSG], F32); gc.append(g)
    stat_mb, g = sb("stat_mb", [128, TSG], F32); gc.append(g)
    spart, g = sb("spart", [128, 64], F32); gc.append(g)
    qpart, g = sb("qpart", [128, 64], F32); gc.append(g)
    c1, g = sb("c1", [128, TSG], F32); gc.append(g)
    c2, g = sb("c2", [128, TSG], F32); gc.append(g)
    c3, g = sb("c3", [128, TSG], F32); gc.append(g)
    c4, g = sb("c4", [128, TSG], F32); gc.append(g)

    bo_bc, g = sb("bo_bc", [1, D], BF16); gc.append(g)
    bff_bc, g = sb("bff_bc", [1, D], BF16); gc.append(g)
    d_bo2 = ir.op("pool", lambda e: e.dma_start(
        out=bo_bc[0:1, :], in_=bo.rearrange("(o d) -> o d", o=1)),
        deps=[b_attn_done], dma=True, chan="st2")
    d_bff2 = ir.op("pool", lambda e: e.dma_start(
        out=bff_bc[0:1, :], in_=bff.rearrange("(o d) -> o d", o=1)),
        deps=[d_bo2], dma=True, chan="st2")
    bias_bc_done = [d_bo2, d_bff2]

    atl = ir.op("sp", lambda e: e.dma_start(
        out=rowbuf[0][:].rearrange("p (c n) -> p c n", c=DC),
        in_=at.rearrange("(c p) n -> p c n", p=128)), deps=[b_attn_done], dma=True, chan="row0")
    atc = ir.op("pool", lambda e: e.tensor_copy(out=at_bf[:], in_=rowbuf[0][:]), deps=[atl])
    btl = ir.op("sp", lambda e: e.dma_start(out=rowbuf[1][:], in_=bt[:]), deps=[b_attn_done], dma=True, chan="row1")
    btc = ir.op("pool", lambda e: e.tensor_copy(out=bt_bf[:], in_=rowbuf[1][:]), deps=[btl])

    rowbuf_user = {0: [atc], 1: [btc]}
    r1w = {}
    r2w = {}
    outlast = {}
    xres_user = {}
    rst_user = {}
    x1c_user = {}
    scr_waw = [None]
    sp_reader = {}
    pl_done = {}
    chain_tail = [None, None]
    last_ffn_mm = [None]
    c2_xc_last = [None]
    xi = [0]

    wcld_prev = {}

    def wstream4(wten, oc, f4):
        i = ws_i[0]; slot = i % 2
        ld = ir.op("pool", lambda e, wten=wten, oc=oc, f4=f4, slot=slot: e.dma_start(
            out=w_bf2[slot][:].rearrange("p (c n) -> p c n", c=4),
            in_=wten[f4 * 512:(f4 + 1) * 512, oc * 512:(oc + 1) * 512].rearrange(
                "(c p) n -> p c n", p=128)),
            deps=[wcld_prev.get(slot), wpe_last.get(i - 2)], dma=True, chan=f"wc{slot}")
        wcld_prev[slot] = ld
        ws_cast[i] = ld
        ws_i[0] += 1
        return ld, i

    def ln_stats_all(pl_deps):
        prev = [p for p in chain_tail if p is not None]
        rs = ir.op("dve", lambda e: e.tensor_reduce(
            out=c1[:], in_=spart[:, 0:TSG * NOC].rearrange("p (t o) -> p t o", t=TSG),
            axis=AX.X, op=ALU.add), deps=list(pl_deps) + prev)
        rq = ir.op("dve", lambda e: e.tensor_reduce(
            out=c2[:], in_=qpart[:, 0:TSG * NOC].rearrange("p (t o) -> p t o", t=TSG),
            axis=AX.X, op=ALU.add), deps=list(pl_deps) + prev)
        for ts in range(TSG):
            sp_reader[ts] = rq
        m0 = ir.op("act", lambda e: e.activation(
            out=stat_mean[:], in_=c1[:], func=AF.Identity, scale=1.0 / D),
            deps=[rs] + prev)
        m1 = ir.op("act", lambda e: e.activation(
            out=c3[:], in_=c2[:], func=AF.Identity, scale=1.0 / D), deps=[rq])
        v0 = ir.op("dve", lambda e: e.tensor_tensor(
            out=c4[:], in0=stat_mean[:], in1=stat_mean[:], op=ALU.mult), deps=[m0])
        v1 = ir.op("dve", lambda e: e.tensor_tensor(
            out=c3[:], in0=c3[:], in1=c4[:], op=ALU.subtract), deps=[m1, v0])
        v2 = ir.op("dve", lambda e: e.tensor_scalar_add(out=c4[:], in0=c3[:], scalar1=EPS),
                   deps=[v1])
        l0 = ir.op("act", lambda e: e.activation(
            out=c3[:], in_=c3[:], func=AF.Ln, bias=eps_sb[:]), deps=[v1, v2, op_eps])
        y0 = ir.op("act", lambda e: e.activation(
            out=c3[:], in_=c3[:], func=AF.Exp, scale=-0.5), deps=[l0])
        n0 = ir.op("dve", lambda e: e.tensor_tensor(out=c1[:], in0=c3[:], in1=c3[:], op=ALU.mult),
                   deps=[y0, rs, m0])
        n1 = ir.op("dve", lambda e: e.tensor_tensor(out=c1[:], in0=c1[:], in1=c4[:], op=ALU.mult),
                   deps=[n0, v2])
        n2 = ir.op("act", lambda e: e.activation(out=c1[:], in_=c1[:], func=AF.Identity,
                                                 scale=-0.5, bias=onep5_sb[:]),
                   deps=[n1, op_1p5])
        n3 = ir.op("dve", lambda e: e.tensor_tensor(
            out=stat_rstd[:], in0=c3[:], in1=c1[:], op=ALU.mult), deps=[n2, y0])
        b0 = ir.op("dve", lambda e: e.tensor_tensor(
            out=c4[:], in0=stat_mean[:], in1=stat_rstd[:], op=ALU.mult), deps=[n3, n1])
        b1 = ir.op("act", lambda e: e.activation(
            out=stat_mb[:], in_=c4[:], func=AF.Identity, scale=-1.0), deps=[b0, n2])
        chain_tail[0], chain_tail[1] = b0, b1
        return [n3, b1]

    out_wr = []
    for tg in range(NTG):
        attnT, g_attnT = sb(f"attnT{tg}", [128, H * TG], BF16)
        pscg = []
        psc = []
        for i in range(TSG):
            t, g = ps(f"psc{tg}_{i}", [128, 512]); pscg.append(g); psc.append(t)
        atld = []
        W2 = H * 512 // 2
        for qh in range(TG // 512):
            qt = tg * (TG // 512) + qh
            for hv in range(2):
                atld.append(ir.op("sp", lambda e, qt=qt, qh=qh, hv=hv: e.dma_start(
                    out=attnT[:, qh * H * 512 + hv * W2:qh * H * 512 + (hv + 1) * W2],
                    in_=a_dram[qt, :, hv * W2:(hv + 1) * W2]),
                    deps=slab_wr_q[qt] + [last_ffn_mm[0]],
                    dma=True, chan=f"atld{len(atld)}"))

        for oc in range(NOC):
            stop_ops = {}
            xl_ops = {}
            for ts in range(TSG):
                i = xi[0] + ts; slot = i % 8
                rows = tg * TG + ts * 128
                xl_ops[ts] = ir.op("act", lambda e, rows=rows, oc=oc, slot=slot: e.dma_start(
                    out=xres[slot][:], in_=x_q[rows:rows + 128, oc * 512:(oc + 1) * 512]),
                    deps=[xres_user.get(slot)], dma=True, chan=f"xr{slot}")
            for ts in range(TSG):
                bmm = pe_op(lambda e, ts=ts, oc=oc: e.matmul(
                    psc[ts][:], orb_sb[:], bo_bc[0:1, oc * 512:(oc + 1) * 512],
                    start=True, stop=False),
                    [d_orb, bank_prev.get(ts), atld[(ts // 4) * 2]] + bias_bc_done)
            for f4 in range(DC // 4):
                cst, wi = wstream4(wo, oc, f4)
                lastmm = None
                for fi in range(4):
                    f = f4 * 4 + fi
                    for ts in range(TSG):
                        deps = [cst, atld[(ts // 4) * 2 + f // (DC // 2)]]
                        acol = (ts // 4) * H * 512 + f * 512 + (ts % 4) * 128
                        mm = pe_op(lambda e, f=f, fi=fi, ts=ts, wi=wi, acol=acol: e.matmul(
                            psc[ts][:], attnT[:, acol:acol + 128],
                            w_bf2[wi % 2][:, fi * 512:(fi + 1) * 512],
                            start=False, stop=(f == DC - 1)), deps)
                        lastmm = mm
                        if f == DC - 1:
                            stop_ops[ts] = mm
                wpe_last[wi] = lastmm
            for ts in range(TSG):
                i = xi[0]; slot = i % 8
                rows = tg * TG + ts * 128
                ev = ir.op("dve", lambda e, ts=ts, slot=slot: e.tensor_tensor(
                    out=rst[slot][:], in0=psc[ts][:], in1=xres[slot][:], op=ALU.add),
                    deps=[stop_ops[ts], xl_ops[ts]] + (rst_user.get(slot) or []))
                bank_prev[ts] = ev
                xres_user[slot] = ev
                wr = ir.op("act", lambda e, rows=rows, oc=oc, slot=slot: e.dma_start(
                    out=r1_dram[rows:rows + 128, oc * 512:(oc + 1) * 512], in_=rst[slot][:]),
                    deps=[ev], dma=True, chan=f"rs{slot}")
                r1w[(oc, ts)] = wr
                rsp = ir.op("dve", lambda e, ts=ts, oc=oc, slot=slot: e.tensor_reduce(
                    out=spart[:, ts * NOC + oc:ts * NOC + oc + 1], in_=rst[slot][:],
                    axis=AX.X, op=ALU.add), deps=[ev, sp_reader.get(ts)])
                rqp = ir.op("act", lambda e, ts=ts, oc=oc, slot=slot: e.activation(
                    out=scr_bf[:, 0:512], in_=rst[slot][:], func=AF.Square,
                    accum_out=qpart[:, ts * NOC + oc:ts * NOC + oc + 1]),
                    deps=[ev, scr_waw[0]], )
                scr_waw[0] = rqp
                rst_user[slot] = [wr, rsp, rqp]
                if oc == NOC - 1:
                    pl_done[ts] = [rsp, rqp]
                xi[0] += 1
        r1_done = [(rst_user.get(s2) or [None])[0] for s2 in range(8)]
        freeall(pscg)
        g_attnT.__exit__(None, None, None)

        x1f, g_x1f = sb(f"x1f{tg}", [128, DC * TG], BF16)
        tp2g = []
        tp2 = []
        for i in range(2):
            t, g = ps(f"tp2{tg}_{i}", [128, 128], BF16); tp2g.append(g); tp2.append(t)
        x1row, g_x1row = sb(f"x1row{tg}", [128, D], BF16)
        x1f_ev = {}
        x1row_waw = [None]
        ln1_fin = ln_stats_all([d for v in pl_done.values() for d in v])
        ln1_done = {ts: ln1_fin for ts in range(TSG)}
        for ts in range(TSG):
            slot = ts % 2
            rows = tg * TG + ts * 128
            ld = ir.op("sp", lambda e, rows=rows, slot=slot: e.dma_start(
                out=rowbuf[slot][:], in_=r1_dram[rows:rows + 128, :]),
                deps=[r1w[(oc2, ts)] for oc2 in range(NOC)] + rowbuf_user.get(slot, []),
                dma=True, chan=f"row{slot}")
            fin = ln1_fin
            xb = ir.op("dve", lambda e, ts=ts, slot=slot: e.tensor_scalar(
                out=x1row[:], in0=rowbuf[slot][:], scalar1=stat_rstd[:, ts:ts + 1],
                scalar2=stat_mb[:, ts:ts + 1], op0=ALU.mult, op1=ALU.add),
                deps=fin + [ld, x1row_waw[0]])
            rowbuf_user[slot] = [xb]
            lasttr = None
            for Dc in range(DC):
                i2 = ts * DC + Dc
                tr = pe_op(lambda e, Dc=Dc, i2=i2: e.transpose(
                    tp2[i2 % 2][:], x1row[:, Dc * 128:(Dc + 1) * 128], id_sb[:]),
                    [xb, d_id, x1f_ev.get(i2 - 2)])
                evv = ir.op("dve", lambda e, Dc=Dc, ts=ts, i2=i2: e.tensor_copy(
                    out=x1f[:, Dc * TG + ts * 128:Dc * TG + (ts + 1) * 128], in_=tp2[i2 % 2][:]),
                    deps=[tr])
                x1f_ev[i2] = evv
                lasttr = tr
            x1row_waw[0] = lasttr
        x1f_done = x1f_ev[TSG * DC - 1]
        psh = []
        pshg = []
        for i in range(2):
            t, g = ps(f"psh{tg}_{i}", [128, 512]); pshg.append(g); psh.append(t)
        h_ev = {}
        for tq in range(TG // 512):
            stop = None
            last_ts = (tq + 1) * 4 - 1
            for Dc in range(DC):
                mm = pe_op(lambda e, Dc=Dc, tq=tq: e.matmul(
                    psh[tq % 2][:], at_bf[:, Dc * 128:(Dc + 1) * 128],
                    x1f[:, Dc * TG + tq * 512:Dc * TG + tq * 512 + 512],
                    start=(Dc == 0), stop=(Dc == DC - 1)),
                    [x1f_ev[last_ts * DC + DC - 1], atc, h_ev.get(tq - 2)])
                stop = mm
            hv = ir.op("act", lambda e, tq=tq: e.activation(
                out=hT_sb[:, tq * 512:tq * 512 + 512], in_=psh[tq % 2][:],
                func=AF.Identity, scale=coef[:]), deps=[stop, op_coef])
            h_ev[tq] = hv
        hT_done = [h_ev[tq] for tq in range(TG // 512)]
        freeall(pshg)
        freeall(tp2g)
        g_x1row.__exit__(None, None, None)

        pscg = []
        psc = []
        for i in range(TSG):
            t, g = ps(f"psd{tg}_{i}", [128, 512]); pscg.append(g); psc.append(t)
        for oc in range(NOC):
            stop_ops = {}
            xl_ops = {}
            for ts in range(TSG):
                i = xi[0] + ts; slot = i % 8
                rows = tg * TG + ts * 128
                rl = ir.op("sp", lambda e, rows=rows, oc=oc, slot=slot: e.dma_start(
                    out=xres[slot][:], in_=r1_dram[rows:rows + 128, oc * 512:(oc + 1) * 512]),
                    deps=[r1w[(oc, ts)], xres_user.get(slot)], dma=True, chan=f"xr{slot}")
                xc = ir.op("act", lambda e, ts=ts, slot=slot: e.activation(
                    out=xres[slot][:], in_=xres[slot][:], func=AF.Identity,
                    scale=stat_rstd[:, ts:ts + 1], bias=stat_mb[:, ts:ts + 1]),
                    deps=[rl] + ln1_done[ts])
                c2_xc_last[0] = xc
                xl_ops[ts] = xc
            for ts in range(TSG):
                bmm = pe_op(lambda e, ts=ts, oc=oc: e.matmul(
                    psc[ts][:], orb_sb[:], bff_bc[0:1, oc * 512:(oc + 1) * 512],
                    start=True, stop=False),
                    [d_orb, bank_prev.get(ts)] + bias_bc_done + (hT_done if oc == 0 else []))
            for f4 in range(DC // 4):
                cst, wi = wstream4(wff, oc, f4)
                lastmm = None
                for fi in range(4):
                    f = f4 * 4 + fi
                    for ts in range(TSG):
                        deps = [cst, x1f_ev[ts * DC + f]]
                        mm = pe_op(lambda e, f=f, fi=fi, ts=ts, wi=wi: e.matmul(
                            psc[ts][:], x1f[:, f * TG + ts * 128:f * TG + (ts + 1) * 128],
                            w_bf2[wi % 2][:, fi * 512:(fi + 1) * 512],
                            start=False, stop=False), deps)
                        lastmm = mm
                wpe_last[wi] = lastmm
            for ts in range(TSG):
                lmm = pe_op(lambda e, ts=ts, oc=oc: e.matmul(
                    psc[ts][:], hT_sb[:, ts * 128:(ts + 1) * 128],
                    bt_bf[:, oc * 512:(oc + 1) * 512], start=False, stop=True),
                    hT_done + [btc])
                stop_ops[ts] = lmm
                last_ffn_mm[0] = lmm
            for ts in range(TSG):
                i = xi[0]; slot = i % 8
                rows = tg * TG + ts * 128
                ev = ir.op("dve", lambda e, ts=ts, slot=slot: e.tensor_tensor(
                    out=rst[slot][:], in0=psc[ts][:], in1=xres[slot][:], op=ALU.add),
                    deps=[stop_ops[ts], xl_ops[ts]] + (rst_user.get(slot) or []))
                bank_prev[ts] = ev
                xres_user[slot] = ev
                wr = ir.op("act", lambda e, rows=rows, oc=oc, slot=slot: e.dma_start(
                    out=r2_dram[rows:rows + 128, oc * 512:(oc + 1) * 512], in_=rst[slot][:]),
                    deps=[ev], dma=True, chan=f"rs{slot}")
                r2w[(oc, ts)] = wr
                rsp = ir.op("dve", lambda e, ts=ts, oc=oc, slot=slot: e.tensor_reduce(
                    out=spart[:, ts * NOC + oc:ts * NOC + oc + 1], in_=rst[slot][:],
                    axis=AX.X, op=ALU.add), deps=[ev, sp_reader.get(ts)])
                rqp = ir.op("act", lambda e, ts=ts, oc=oc, slot=slot: e.activation(
                    out=scr_bf[:, 0:512], in_=rst[slot][:], func=AF.Square,
                    accum_out=qpart[:, ts * NOC + oc:ts * NOC + oc + 1]),
                    deps=[ev, scr_waw[0]], )
                scr_waw[0] = rqp
                rst_user[slot] = [wr, rsp, rqp]
                if oc == NOC - 1:
                    pl_done[ts] = [rsp, rqp]
                xi[0] += 1
        r2_done = [(rst_user.get(s2) or [None])[0] for s2 in range(8)]
        freeall(pscg)
        g_x1f.__exit__(None, None, None)

        ln2_fin = ln_stats_all([d for v in pl_done.values() for d in v] + [c2_xc_last[0]])
        D2 = D // 2
        for ts in range(TSG):
            slot = ts % 2
            rows = tg * TG + ts * 128
            lds = []
            ldeps = [r2w[(oc2, ts)] for oc2 in range(NOC)]
            for hv in range(2):
                lds.append(ir.op("act", lambda e, rows=rows, slot=slot, hv=hv: e.dma_start(
                    out=rowbuf[slot][:, hv * D2:(hv + 1) * D2],
                    in_=r2_dram[rows:rows + 128, hv * D2:(hv + 1) * D2]),
                    deps=ldeps + [c2_xc_last[0]] + rowbuf_user.get(slot, []),
                    dma=True, chan=f"row{slot}" if hv == 0 else f"row{slot}b"))
            fin = ln2_fin
            oa2 = []
            D4 = D // 4
            for qv in range(4):
                osl = (ts * 4 + qv) % 5
                oa = ir.op("dve", lambda e, ts=ts, slot=slot, qv=qv, osl=osl, D4=D4: e.tensor_scalar(
                    out=outst[osl][:], in0=rowbuf[slot][:, qv * D4:(qv + 1) * D4],
                    scalar1=stat_rstd[:, ts:ts + 1],
                    scalar2=stat_mb[:, ts:ts + 1], op0=ALU.mult, op1=ALU.add),
                    deps=fin + [lds[qv // 2]] + (outlast.get(osl) or []))
                oa2.append(oa)
                ow = ir.op("act", lambda e, rows=rows, qv=qv, osl=osl, D4=D4: e.dma_start(
                    out=y[rows:rows + 128, qv * D4:(qv + 1) * D4],
                    in_=outst[osl][:]),
                    deps=[oa], dma=True, chan=f"out{osl}")
                outlast[osl] = [ow]
                out_wr.append(ow)
            rowbuf_user[slot] = oa2 + fin

    ir.op("sp", None, deps=out_wr[-5:])

    ir.resolve(nc, stack)
    ir.emit(nc)
    stack.close()
    freeall(gc)
    freeall(guards)
    nc.finalize()
    return nc



def make_core_inputs(cfg, x, wq, bq, wk, bk, wv, bv, wo, bo, la, lb, wff, bff, lwts, b, half):
    T, D = cfg["T"], cfg["D"]
    blocks = half_blocks(T, half)
    idx = np.concatenate([np.arange(bb * 128, (bb + 1) * 128) for bb in blocks])
    xb = np.asarray(x[b], np.float32)
    xt = np.ascontiguousarray(xb.T)
    NR = int(np.asarray(la).shape[0]) * int(np.asarray(la).shape[1])
    d = dict(
        xt_kv=xt,
        xt_q=np.ascontiguousarray(xt[:, idx]),
        x_q=np.ascontiguousarray(xb[idx]),
        wq=np.asarray(wq, np.float32), wk=np.asarray(wk, np.float32),
        wv=np.asarray(wv, np.float32), wo=np.asarray(wo, np.float32),
        wff=np.ascontiguousarray(np.asarray(wff, np.float32).T),
        at=np.ascontiguousarray(np.asarray(la, np.float32).transpose(2, 0, 1).reshape(D, NR)),
        bt=np.ascontiguousarray(np.asarray(lb, np.float32).transpose(0, 2, 1).reshape(NR, D)),
        bq=np.asarray(bq, np.float32), bk=np.asarray(bk, np.float32),
        bv=np.asarray(bv, np.float32), bo=np.asarray(bo, np.float32),
        bff=np.asarray(bff, np.float32),
        lw=np.repeat(np.asarray(lwts, np.float32), int(np.asarray(la).shape[1])),
        masks=build_masks(cfg, half),
        ident=np.eye(128, dtype=BF16NP),
        onescol=np.ones((128, 1), BF16NP),
        onesrowf=np.ones((1, 128), np.float32),
        onesrowb=np.ones((1, 128), BF16NP),
    )
    return d, idx


def kernel(L0_attn_x, L0_attn_Wq, L0_attn_bq, L0_attn_Wk, L0_attn_bk,
           L0_attn_Wv, L0_attn_bv, L0_attn_Wo, L0_attn_bo,
           L0_ffn_lora_A, L0_ffn_lora_B, L0_ffn_W, L0_ffn_bias, L0_ffn_lora_weights):
    cfg = real_cfg()
    B = 4
    nc = build_program(cfg)
    in_maps = []
    idxs = []
    for c in range(8):
        b, half = c // 2, c % 2
        d, idx = make_core_inputs(cfg, L0_attn_x, L0_attn_Wq, L0_attn_bq,
                                  L0_attn_Wk, L0_attn_bk, L0_attn_Wv, L0_attn_bv,
                                  L0_attn_Wo, L0_attn_bo, L0_ffn_lora_A, L0_ffn_lora_B,
                                  L0_ffn_W, L0_ffn_bias, L0_ffn_lora_weights, b, half)
        in_maps.append(d)
        idxs.append((b, idx))
    res = run_bass_kernel_spmd(nc, in_maps, core_ids=list(range(8)))
    out = np.empty((B, cfg["T"], cfg["D"]), np.float32)
    for c in range(8):
        b, idx = idxs[c]
        out[b, idx] = np.asarray(res.results[c]["y"], np.float32)
    return out


# version 30
# speedup vs baseline: 1.1681x; 1.1681x over previous
import numpy as np
import ml_dtypes
from contextlib import ExitStack

import concourse.bass as bass
import concourse.mybir as mybir
from concourse.bass_utils import run_bass_kernel_spmd

F32 = mybir.dt.float32
BF16 = mybir.dt.bfloat16
AF = mybir.ActivationFunctionType
ALU = mybir.AluOpType
AX = mybir.AxisListType
BF16NP = ml_dtypes.bfloat16


def real_cfg():
    return dict(T=4096, D=4096, H=32, KVH=8, NQ=2048, GT=2048, TG=1024)


def half_blocks(T, half):
    nw = T // 1024
    if half == 0:
        return [8 * j + d for j in range(nw) for d in (0, 3, 4, 7)]
    return [8 * j + d for j in range(nw) for d in (1, 2, 5, 6)]


def build_masks(cfg, half):
    T, NQ = cfg["T"], cfg["NQ"]
    NQT = NQ // 512
    blocks = half_blocks(T, half)
    m = np.zeros((NQT, 8, 128, 128), np.float32)
    for j in range(NQT):
        gq = np.concatenate([np.arange(b * 128, (b + 1) * 128) for b in blocks[4 * j:4 * j + 4]])
        for c in range(8):
            kk = (8 * j + c) * 128 + np.arange(128)
            qq = gq[(c // 2) * 128:(c // 2) * 128 + 128]
            m[j, c] = (kk[:, None] <= qq[None, :]).astype(np.float32)
    return m.astype(BF16NP)



class IR:

    def __init__(self):
        self.ops = []
        self.phases = []
        self._cur = None
        self._last = {}
        self._barrier = []

    def phase(self, name):
        self._barrier = sorted(self._last.values())
        self._cur = []
        self.phases.append((name, self._cur))

    def op(self, eng, fn, deps=(), dma=False, chan=None):
        i = len(self.ops)
        assert (chan is not None) == dma, "dma ops need a chan; compute ops must not"
        self.ops.append(dict(eng=eng, fn=fn, dma=dma, chan=chan,
                             deps=sorted({d for d in deps if d is not None} | set(self._barrier)),
                             sig=False, sem=None, val=None))
        self._cur.append(i)
        self._last[chan if dma else eng] = i
        return i

    def resolve(self, nc, ctx):
        for o in self.ops:
            for d in o["deps"]:
                self.ops[d]["sig"] = True
        sems, counts = {}, {}
        for pi, (pname, opids) in enumerate(self.phases):
            for i in opids:
                o = self.ops[i]
                if not (o["sig"] or o["dma"]):
                    continue
                if o["dma"]:
                    key = ("d", o["chan"])
                else:
                    key = ("c", o["eng"], pi)
                if key not in sems:
                    sems[key] = ctx.enter_context(nc.semaphore("_".join(str(k) for k in key)))
                    counts[key] = 0
                counts[key] += 16 if o["dma"] else 1
                o["sem"], o["val"] = sems[key], counts[key]

    def emit(self, nc):
        engmap = dict(sp="sync", act="scalar", dve="vector", pe="tensor", pool="gpsimd")
        for pi, (pname, opids) in enumerate(self.phases):
            byeng = {}
            for i in opids:
                byeng.setdefault(self.ops[i]["eng"], []).append(i)
            with nc.Block(pname) as block:
                for eng, ids in byeng.items():
                    def body(e, ids=ids):
                        waited = {}
                        for i in ids:
                            o = self.ops[i]
                            for d in o["deps"]:
                                p = self.ops[d]
                                if waited.get(p["sem"].num, -1) < p["val"]:
                                    e.wait_ge(p["sem"], p["val"])
                                    waited[p["sem"].num] = p["val"]
                            if o["fn"] is not None:
                                inst = o["fn"](e)
                                if o["sig"] or o["dma"]:
                                    inst.then_inc(o["sem"], 16 if o["dma"] else 1)
                    getattr(block, engmap[eng])(body)



def build_program(cfg):
    T, D, H, KVH = cfg["T"], cfg["D"], cfg["H"], cfg["KVH"]
    NQ, GT, TG = cfg["NQ"], cfg["GT"], cfg["TG"]
    DC = D // 128
    DKV = KVH * 128
    REP = H // KVH
    NG = T // GT
    NSUB = GT // 512
    NSUBQ = NQ // 512
    NQT = NQ // 512
    NTG = NQ // TG
    TSG = TG // 128
    NOC = D // 512
    NKC = T // 128
    ISQ = 1.0 / float(np.sqrt(128.0))
    EPS = 1e-5
    SCALE = 2.0

    nc = bass.Bass("TRN2", target_bir_lowering=False, debug=False)

    def din(name, shape, dt=F32):
        return nc.dram_tensor(name, shape, dt, kind="ExternalInput").ap()

    xt_kv = din("xt_kv", [D, T]); xt_q = din("xt_q", [D, NQ]); x_q = din("x_q", [NQ, D])
    wq = din("wq", [D, D]); wk = din("wk", [D, DKV]); wv = din("wv", [D, DKV])
    wo = din("wo", [D, D]); wff = din("wff", [D, D])
    at = din("at", [D, 128]); bt = din("bt", [128, D])
    bq = din("bq", [D]); bk = din("bk", [DKV]); bv = din("bv", [DKV])
    bo = din("bo", [D]); bff = din("bff", [D]); lw = din("lw", [128])
    masks = din("masks", [NQT, 8, 128, 128], BF16)
    ident = din("ident", [128, 128], BF16)
    onescol = din("onescol", [128, 1], BF16)
    onesrowf = din("onesrowf", [1, 128])
    onesrowb = din("onesrowb", [1, 128], BF16)
    y = nc.dram_tensor("y", [NQ, D], F32, kind="ExternalOutput").ap()

    k_dram = nc.dram_tensor("k_scr", [KVH, 128, T], BF16, kind="Internal").ap()
    v_dram = nc.dram_tensor("v_scr", [KVH, 128, T], BF16, kind="Internal").ap()
    q_dram = nc.dram_tensor("q_scr", [H, 128, NQ], BF16, kind="Internal").ap()
    a_dram = nc.dram_tensor("a_scr", [NQT, 128, H * 512], BF16, kind="Internal").ap()
    r1_dram = nc.dram_tensor("r1_scr", [NQ, D], F32, kind="Internal").ap()
    r2_dram = nc.dram_tensor("r2_scr", [NQ, D], F32, kind="Internal").ap()

    ir = IR()
    stack = ExitStack()
    guards = []

    def sb(name, shape, dt):
        g = nc.sbuf_tensor(name, shape, dt)
        return g.__enter__(), g

    def ps(name, shape, dt=F32):
        g = nc.psum_tensor(name, shape, dt)
        return g.__enter__(), g

    def freeall(gs):
        for g in reversed(gs):
            g.__exit__(None, None, None)

    def sbp(name, shape, dt):
        t, g = sb(name, shape, dt)
        guards.append(g)
        return t

    bk_sb = sbp("bk_sb", [128, KVH], F32)
    bv_sb = sbp("bv_sb", [128, KVH], F32)
    bq_sb = sbp("bq_sb", [128, H], F32)
    lw_sb = sbp("lw_sb", [128, 1], F32)
    coef = sbp("coef", [128, 1], F32)
    eps_sb = sbp("eps_sb", [128, 1], F32)
    onep5_sb = sbp("onep5_sb", [128, 1], F32)
    id_sb = sbp("id_sb", [128, 128], BF16)
    ocol_sb = sbp("ocol_sb", [128, 1], BF16)
    orf_sb = sbp("orf_sb", [1, 128], F32)
    orb_sb = sbp("orb_sb", [1, 128], BF16)

    ir.phase("setup")
    NCslow = dict(allow_slow_non_contiguous=True)
    d_bk = ir.op("sp", lambda e: e.dma_start(out=bk_sb[:], in_=bk.rearrange("(i p) -> p i", p=128), **NCslow), dma=True, chan="st")
    d_bv = ir.op("sp", lambda e: e.dma_start(out=bv_sb[:], in_=bv.rearrange("(i p) -> p i", p=128), **NCslow), deps=[d_bk], dma=True, chan="st")
    d_bq = ir.op("sp", lambda e: e.dma_start(out=bq_sb[:], in_=bq.rearrange("(i p) -> p i", p=128), **NCslow), deps=[d_bv], dma=True, chan="st")
    d_lw = ir.op("sp", lambda e: e.dma_start(out=lw_sb[:], in_=lw.rearrange("(p o) -> p o", o=1), **NCslow), deps=[d_bq], dma=True, chan="st")
    d_id = ir.op("sp", lambda e: e.dma_start(out=id_sb[:], in_=ident[:]), deps=[d_lw], dma=True, chan="st")
    d_oc = ir.op("sp", lambda e: e.dma_start(out=ocol_sb[:], in_=onescol[:]), deps=[d_id], dma=True, chan="st")
    d_orf = ir.op("sp", lambda e: e.dma_start(out=orf_sb[:], in_=onesrowf[:]), deps=[d_oc], dma=True, chan="st")
    d_orb = ir.op("sp", lambda e: e.dma_start(out=orb_sb[:], in_=onesrowb[:]), deps=[d_orf], dma=True, chan="st")
    biasdeps = [d_bq]
    op_coef = ir.op("act", lambda e: e.activation(out=coef[:], in_=lw_sb[:], func=AF.Identity, scale=SCALE), deps=[d_lw])
    op_eps = ir.op("dve", lambda e: e.memset(eps_sb[:], EPS))
    op_1p5 = ir.op("dve", lambda e: e.memset(onep5_sb[:], 1.5))

    ir.phase("qkv")
    ga = []
    GX = max(GT, NQ)
    x_bf, g = sb("x_bf", [128, DC * GX], BF16); ga.append(g)
    w_bf, g = sb("w_bf", [128, DC * 512], BF16); ga.append(g)
    xstage = []
    for i in range(2):
        t, g = sb(f"xstage{i}", [128, GX], F32); ga.append(g); xstage.append(t)
    estage = []
    for i in range(4):
        t, g = sb(f"estage{i}", [128, 512], BF16); ga.append(g); estage.append(t)
    pa = []
    pag = []
    for i in range(8):
        t, g = ps(f"pa{i}", [128, 512]); pag.append(g); pa.append(t)

    kv_chunks = ([("k", wk, i, bk_sb, i, k_dram) for i in range(KVH)] +
                 [("v", wv, i, bv_sb, i, v_dram) for i in range(KVH)])
    q_chunks = [("q", wq, i, bq_sb, i, q_dram) for i in range(H)]

    last_wr = {"k": None, "v": None, "q": None}
    bank_prev = {}
    est_wr = {}
    ws_cast = {}
    wpe_last = {}
    xld_prev = [None, None, None, None]
    wld_prev = [None, None]
    wa_use = {}
    w_loaded = [None]
    w_saved = [None]
    pe_last_half = {0: None, 1: None}
    pe_last_half_cur = {0: None, 1: None}
    ev_i, ws_i = [0], [0]
    last_pe_op = [None]

    def pe_op(fn, deps):
        i = ir.op("pe", fn, deps)
        last_pe_op[0] = i
        return i

    x_bf3 = x_bf[:].rearrange("p (c t) -> p c t", c=DC)
    w_bf3 = w_bf[:].rearrange("p (c n) -> p c n", c=DC)

    def gen_gemm(src, gcount, gt, nsub, chunks, oc_base, same_layout=False):
        halves = 2 if nsub >= 2 else 1
        gth = gt // halves
        nsh = nsub // halves
        x3loc = x_bf[:].rearrange("p (c t) -> p c t", c=DC) if gt == GX else \
            x_bf[:, 0:DC * gt].rearrange("p (c t) -> p c t", c=DC)
        for gi in range(gcount):
            xc_ops = {}
            for half in range(halves):
                hdeps = [pe_last_half[half]] if (gi > 0 or same_layout) \
                    else [pe_last_half[0], pe_last_half[1]]
                for Dc2 in range(DC // 2):
                    Dc = 2 * Dc2
                    ld = ir.op("pool", lambda e, Dc=Dc, gi=gi, half=half, src=src, gt=gt, gth=gth, x3loc=x3loc: e.dma_start(
                        out=x3loc[:, Dc:Dc + 2, half * gth:(half + 1) * gth],
                        in_=src[Dc * 128:(Dc + 2) * 128,
                                gi * gt + half * gth:gi * gt + (half + 1) * gth].rearrange(
                                    "(c p) t -> p c t", p=128)),
                        deps=[xld_prev[Dc2 % 4]] + hdeps, dma=True, chan=f"xa{Dc2 % 4}")
                    xld_prev[Dc2 % 4] = ld
                    xc_ops[(half, Dc)] = ld
                    xc_ops[(half, Dc + 1)] = ld
            slabs = []
            for kind in ("k", "v", "q"):
                kc_ = [ch for ch in chunks if ch[0] == kind]
                slabs += [kc_[i:i + 4] for i in range(0, len(kc_), 4)]
            if gi % 2 == 1:
                slabs = slabs[::-1]
            for slab in slabs:
                width = len(slab) * 128
                wten, col0 = slab[0][1], slab[0][2] * 128
                skey = (slab[0][0], slab[0][2], width, id(wten))
                if w_loaded[0] == skey:
                    wc_ops = w_saved[0]
                else:
                    wc_ops = {}
                    for Dc4 in range(DC // 4):
                        dset = list(range(Dc4 * 4, Dc4 * 4 + 4))
                        ld = ir.op("pool", lambda e, wten=wten, Dc4=Dc4, col0=col0, width=width: e.dma_start(
                            out=w_bf3[:, Dc4 * 4:(Dc4 + 1) * 4, 0:width],
                            in_=wten[Dc4 * 512:(Dc4 + 1) * 512, col0:col0 + width].rearrange(
                                "(c p) n -> p c n", p=128)),
                            deps=[wld_prev[Dc4 % 2]] + [wa_use.get(d) for d in dset],
                            dma=True, chan=f"wa{Dc4 % 2}")
                        wld_prev[Dc4 % 2] = ld
                        for d in dset:
                            wc_ops[d] = ld
                new_lastuse = {}
                is_first_after = (slab is slabs[0]) and (gi > 0 or same_layout)
                if (slab is slabs[-1]) or is_first_after:
                    pairs = [(ci, half) for half in range(halves) for ci in range(len(slab))]
                else:
                    pairs = [(ci, half) for ci in range(len(slab)) for half in range(halves)]
                for ci, half in pairs:
                    kind, _w, chunk_i, bias_sb, bias_i, dest = slab[ci]
                    par = ci % 2
                    stop_ops = {}
                    for Dc in range(DC):
                        for sub in range(nsh):
                            bank = par * 4 + half * nsh + sub
                            deps = [wc_ops[Dc], xc_ops[(half, Dc)]]
                            if Dc == 0:
                                deps.append(bank_prev.get(bank))
                            mm = pe_op(lambda e, bank=bank, Dc=Dc, ci=ci, half=half, sub=sub, gt=gt, gth=gth: e.matmul(
                                pa[bank][:],
                                w_bf[:, Dc * 512 + ci * 128:Dc * 512 + (ci + 1) * 128],
                                x_bf[:, Dc * gt + half * gth + sub * 512:Dc * gt + half * gth + sub * 512 + 512],
                                start=(Dc == 0), stop=(Dc == DC - 1)), deps)
                            if Dc == DC - 1:
                                stop_ops[sub] = mm
                            pe_last_half_cur[half] = mm
                            new_lastuse[Dc] = mm
                    for sub in range(nsh):
                        bank = par * 4 + half * nsh + sub
                        i = ev_i[0]; slot = i % 4
                        act = ir.op("act", lambda e, bank=bank, slot=slot, bias_sb=bias_sb, bias_i=bias_i: e.activation(
                            out=estage[slot][:], in_=pa[bank][:], func=AF.Identity,
                            bias=bias_sb[:, bias_i:bias_i + 1]),
                            deps=[stop_ops[sub], est_wr.get(i - 4)] + biasdeps)
                        bank_prev[bank] = act
                        if kind == "q":
                            c0 = half * gth + sub * 512
                        else:
                            c0 = gi * gt + half * gth + sub * 512
                        wr = ir.op("act", lambda e, dest=dest, chunk_i=chunk_i, c0=c0, slot=slot: e.dma_start(
                            out=dest[chunk_i, :, c0:c0 + 512], in_=estage[slot][:]),
                            deps=[act], dma=True, chan=f"es{slot}")
                        est_wr[i] = wr
                        last_wr[kind] = wr
                        ev_i[0] += 1
                for d in range(DC):
                    wa_use[d] = new_lastuse[d]
                w_loaded[0] = skey
                w_saved[0] = wc_ops
            pe_last_half[0] = pe_last_half_cur[0]
            pe_last_half[1] = pe_last_half_cur[1]

    gen_gemm(xt_kv, NG, GT, NSUB, kv_chunks, 0)
    gen_gemm(xt_q, 1, NQ, NSUBQ, q_chunks, len(kv_chunks),
             same_layout=(NQ == GT))
    a_done = [last_wr["k"], last_wr["v"], last_wr["q"]]
    freeall(pag)
    freeall(ga)

    ir.phase("attn")
    gb = []
    k_sb, g = sb("k_sb", [128, KVH * T], BF16); gb.append(g)
    v_sb, g = sb("v_sb", [128, NKC * DKV], BF16); gb.append(g)
    q_sb = []
    for i in range(2):
        t, g = sb(f"q_sb{i}", [128, 512], BF16); gb.append(g); q_sb.append(t)
    pt_sb = []
    for i in range(4):
        t, g = sb(f"pt_sb{i}", [128, 512], BF16); gb.append(g); pt_sb.append(t)
    mask_sb, g = sb("mask_sb", [128, 8 * 128], BF16); gb.append(g)
    aslab, g = sb("aslab", [128, H * 512], BF16); gb.append(g)
    t, g = sb("rt_sb0", [128, 512], F32); gb.append(g)
    rt_sb = [t, t]
    t, g = sb("den_sb0", [1, 512], F32); gb.append(g)
    den_sb = [t, t]
    t, g = sb("rec_sb0", [1, 512], F32); gb.append(g)
    rec_sb = [t, t]

    kld = {}
    vtld = {}
    vt_prev = {}

    def need_k(kv):
        if kv not in kld:
            kld[kv] = ir.op("sp", lambda e, kv=kv: e.dma_start(
                out=k_sb[:, kv * T:(kv + 1) * T], in_=k_dram[kv]),
                deps=a_done + [kld.get(kv - 1)], dma=True, chan="kld")
        return kld[kv]

    def need_v(kv, kc):
        if (kv, kc) not in vtld:
            vtld[(kv, kc)] = ir.op("sp", lambda e, kv=kv, kc=kc: e.dma_start(
                out=v_sb[:, kc * DKV + kv * 128:kc * DKV + (kv + 1) * 128],
                in_=v_dram[kv, :, kc * 128:(kc + 1) * 128], transpose=True),
                deps=a_done + [vt_prev.get(kv)], dma=True, chan=f"vt{kv}")
            vt_prev[kv] = vtld[(kv, kc)]
        return vtld[(kv, kc)]

    psg = []
    def ps2(name):
        t, g = ps(name, [128, 512]); psg.append(g); return t
    ps_s = [ps2(f"ps_s{i}") for i in range(4)]
    ps_av = [ps2(f"ps_av{i}") for i in range(2)]
    t, g = ps("ps_den0", [1, 512]); psg.append(g)
    ps_den = [t, t]
    ps_rt = [ps2("ps_rt0")] * 2

    ptlast = {}
    sfree = {}
    qlast = {}
    at_evac = {}
    rt_evac = {}
    dclast = {}
    lnlast = {}
    rtmm_last = {}
    av_par = {}
    tail_prev = [None]
    swq = {}
    slab_wr_q = {}
    QD = max(H // 2, 1)
    mask_last = [None]
    b_attn_done = None

    for qt in range(NQT):
        NK = 8 * (qt + 1)
        mld = ir.op("sp", lambda e, qt=qt: e.dma_start(
            out=mask_sb[:].rearrange("p (c q) -> p c q", c=8),
            in_=masks[qt].rearrange("c p q -> p c q")),
            deps=a_done + [mask_last[0]], dma=True, chan="mld")
        LAG = 3
        queue = []
        tails = []

        def pump():
            fn = queue.pop(0)
            fn()
            for tset in tails:
                tset[0] -= 1
            while tails and tails[0][0] <= 0:
                tails.pop(0)[1]()

        for h in range(H):
            kv = h // REP
            hp = h % 2
            ql = ir.op("sp", lambda e, h=h, qt=qt, hp=hp: e.dma_start(
                out=q_sb[hp][:], in_=q_dram[h, :, qt * 512:(qt + 1) * 512]),
                deps=[a_done[2], qlast.get(hp)], dma=True, chan=f"ql{hp}")
            exp_cur = {}
            stopv = {}

            def srange(kc, NK=NK):
                if kc >= NK - 8:
                    return 128 * ((kc - (NK - 8)) // 2)
                return 0

            def mk_avden(kc, h=h, kv=kv, hp=hp, NK=NK, exp_cur=exp_cur, stopv=stopv, srange=srange):
                def go():
                    s = srange(kc)
                    par = kc % 4
                    cur = exp_cur[kc]
                    avdeps = [cur, need_v(kv, kc)]
                    if kc == 0:
                        avdeps += [av_par.get(hp), dclast.get(0)]
                    av = pe_op(lambda e, kc=kc, kv=kv, par=par, hp=hp, NK=NK, s=s: e.matmul(
                        ps_av[hp][:, s:512], v_sb[:, kc * DKV + kv * 128:kc * DKV + (kv + 1) * 128],
                        pt_sb[par][:, s:512], start=(kc == 0), stop=(kc == NK - 1)), avdeps)
                    dn = pe_op(lambda e, kc=kc, par=par, hp=hp, NK=NK, s=s: e.matmul(
                        ps_den[hp][:, s:512], ocol_sb[:], pt_sb[par][:, s:512],
                        start=(kc == 0), stop=(kc == NK - 1)), [cur, d_oc])
                    ptlast[par] = dn
                    if kc == NK - 1:
                        stopv[0], stopv[1] = av, dn
                return go

            for kc in range(NK):
                par = kc % 4
                s = srange(kc)
                mm = pe_op(lambda e, kc=kc, kv=kv, par=par, hp=hp, s=s: e.matmul(
                    ps_s[par][:, s:512], k_sb[:, kv * T + kc * 128:kv * T + (kc + 1) * 128],
                    q_sb[hp][:, s:512], start=True, stop=True),
                    [ql, need_k(kv), sfree.get(par)])
                qlast[hp] = mm
                ex = ir.op("act", lambda e, par=par, s=s: e.activation(
                    out=pt_sb[par][:, s:512], in_=ps_s[par][:, s:512], func=AF.Exp, scale=ISQ),
                    deps=[mm, ptlast.get(par)])
                sfree[par] = ex
                cur = ex
                if kc >= NK - 8:
                    c = kc - (NK - 8)
                    cur = ir.op("dve", lambda e, par=par, c=c, s=s: e.tensor_tensor(
                        out=pt_sb[par][:, s:s + 128], in0=pt_sb[par][:, s:s + 128],
                        in1=mask_sb[:, c * 128:(c + 1) * 128], op=ALU.mult),
                        deps=[ex, mld])
                    mask_last[0] = cur
                exp_cur[kc] = cur
                queue.append(mk_avden(kc))
                if len(queue) > LAG:
                    pump()

            def make_tails(h=h, hp=hp, stopv=stopv):
                carry = {}

                def tail1():
                    stopDen = stopv[1]
                    dc = ir.op("dve", lambda e, hp=hp: e.tensor_copy(
                        out=den_sb[hp][:], in_=ps_den[hp][:]), deps=[stopDen, lnlast.get(0)])
                    dclast[0] = dc
                    ln1 = ir.op("act", lambda e, hp=hp: e.activation(
                        out=rec_sb[hp][:], in_=den_sb[hp][:], func=AF.Ln), deps=[dc, rtmm_last.get(0)])
                    dr = ir.op("act", lambda e, hp=hp: e.activation(
                        out=rec_sb[hp][:], in_=rec_sb[hp][:], func=AF.Exp, scale=-1.0),
                        deps=[ln1])
                    lnlast[0] = dr
                    carry["dr"] = dr

                def tail2(qt=qt):
                    stopAV = stopv[0]
                    rt = pe_op(lambda e, hp=hp: e.matmul(
                        ps_rt[hp][:], orf_sb[:], rec_sb[hp][:], start=True, stop=True),
                        [carry["dr"], d_orf, rt_evac.get(0)])
                    rtmm_last[0] = rt
                    re = ir.op("act", lambda e, hp=hp: e.activation(
                        out=rt_sb[hp][:], in_=ps_rt[hp][:], func=AF.Identity),
                        deps=[rt, at_evac.get(0)])
                    rt_evac[0] = re
                    qd = h // QD
                    ae = ir.op("dve", lambda e, h=h, hp=hp: e.tensor_tensor(
                        out=aslab[:, h * 512:(h + 1) * 512], in0=ps_av[hp][:],
                        in1=rt_sb[hp][:], op=ALU.mult), deps=[stopAV, re, swq.get(qd)])
                    av_par[hp] = ae
                    at_evac[0] = ae
                return tail1, tail2
            t1, t2 = make_tails()
            tails.append([LAG, t1])
            tails.append([LAG + 2, t2])
        while queue:
            pump()
        for tset in tails:
            tset[1]()
        tails.clear()
        sws = []
        for qd in range(2):
            w = ir.op("sp", lambda e, qt=qt, qd=qd: e.dma_start(
                out=a_dram[qt, :, qd * QD * 512:(qd + 1) * QD * 512],
                in_=aslab[:, qd * QD * 512:(qd + 1) * QD * 512]),
                deps=[av_par.get(0), av_par.get(1), swq.get(qd)],
                dma=True, chan=f"sw{qd}")
            swq[qd] = w
            sws.append(w)
        slab_wr_q[qt] = sws
        b_attn_done = sws[-1]
    freeall(psg)
    freeall(gb)

    ir.phase("post")
    gc = []
    at_bf, g = sb("at_bf", [128, DC * 128], BF16); gc.append(g)
    bt_bf, g = sb("bt_bf", [128, D], BF16); gc.append(g)
    rowbuf = []
    for i in range(2):
        t, g = sb(f"rowbuf{i}", [128, D], F32); gc.append(g); rowbuf.append(t)
    outst = []
    for i in range(5):
        t, g = sb(f"outst{i}", [128, D // 4], F32); gc.append(g); outst.append(t)
    scr_bf, g = sb("scr_bf", [128, 512], BF16); gc.append(g)
    hT_sb, g = sb("hT_sb", [128, TG], BF16); gc.append(g)
    w_bf2 = []
    for i in range(2):
        t, g = sb(f"w_bf2{i}", [128, 4 * 512], BF16); gc.append(g); w_bf2.append(t)
    xres = []
    for i in range(8):
        t, g = sb(f"xres{i}", [128, 512], F32); gc.append(g); xres.append(t)
    rst = []
    for i in range(8):
        t, g = sb(f"rst{i}", [128, 512], F32); gc.append(g); rst.append(t)
    stat_mean, g = sb("stat_mean", [128, TSG], F32); gc.append(g)
    stat_rstd, g = sb("stat_rstd", [128, TSG], F32); gc.append(g)
    stat_mb, g = sb("stat_mb", [128, TSG], F32); gc.append(g)
    spart, g = sb("spart", [128, 64], F32); gc.append(g)
    qpart, g = sb("qpart", [128, 64], F32); gc.append(g)
    c1, g = sb("c1", [128, TSG], F32); gc.append(g)
    c2, g = sb("c2", [128, TSG], F32); gc.append(g)
    c3, g = sb("c3", [128, TSG], F32); gc.append(g)
    c4, g = sb("c4", [128, TSG], F32); gc.append(g)

    bo_bc, g = sb("bo_bc", [1, D], BF16); gc.append(g)
    bff_bc, g = sb("bff_bc", [1, D], BF16); gc.append(g)
    d_bo2 = ir.op("pool", lambda e: e.dma_start(
        out=bo_bc[0:1, :], in_=bo.rearrange("(o d) -> o d", o=1)),
        deps=[b_attn_done], dma=True, chan="st2")
    d_bff2 = ir.op("pool", lambda e: e.dma_start(
        out=bff_bc[0:1, :], in_=bff.rearrange("(o d) -> o d", o=1)),
        deps=[d_bo2], dma=True, chan="st2")
    bias_bc_done = [d_bo2, d_bff2]

    atl = ir.op("sp", lambda e: e.dma_start(
        out=rowbuf[0][:].rearrange("p (c n) -> p c n", c=DC),
        in_=at.rearrange("(c p) n -> p c n", p=128)), deps=[b_attn_done], dma=True, chan="row0")
    atc = ir.op("pool", lambda e: e.tensor_copy(out=at_bf[:], in_=rowbuf[0][:]), deps=[atl])
    btl = ir.op("sp", lambda e: e.dma_start(out=rowbuf[1][:], in_=bt[:]), deps=[b_attn_done], dma=True, chan="row1")
    btc = ir.op("pool", lambda e: e.tensor_copy(out=bt_bf[:], in_=rowbuf[1][:]), deps=[btl])

    rowbuf_user = {0: [atc], 1: [btc]}
    r1w = {}
    r2w = {}
    outlast = {}
    xres_user = {}
    rst_user = {}
    x1c_user = {}
    scr_waw = [None]
    sp_reader = {}
    pl_done = {}
    chain_tail = [None, None]
    last_ffn_mm = [None]
    c2_xc_last = [None]
    xi = [0]

    wcld_prev = {}

    def wstream4(wten, oc, f4):
        i = ws_i[0]; slot = i % 2
        ld = ir.op("pool", lambda e, wten=wten, oc=oc, f4=f4, slot=slot: e.dma_start(
            out=w_bf2[slot][:].rearrange("p (c n) -> p c n", c=4),
            in_=wten[f4 * 512:(f4 + 1) * 512, oc * 512:(oc + 1) * 512].rearrange(
                "(c p) n -> p c n", p=128)),
            deps=[wcld_prev.get(slot), wpe_last.get(i - 2)], dma=True, chan=f"wc{slot}")
        wcld_prev[slot] = ld
        ws_cast[i] = ld
        ws_i[0] += 1
        return ld, i

    def ln_stats_all(pl_deps):
        prev = [p for p in chain_tail if p is not None]
        rs = ir.op("dve", lambda e: e.tensor_reduce(
            out=c1[:], in_=spart[:, 0:TSG * NOC].rearrange("p (t o) -> p t o", t=TSG),
            axis=AX.X, op=ALU.add), deps=list(pl_deps) + prev)
        rq = ir.op("dve", lambda e: e.tensor_reduce(
            out=c2[:], in_=qpart[:, 0:TSG * NOC].rearrange("p (t o) -> p t o", t=TSG),
            axis=AX.X, op=ALU.add), deps=list(pl_deps) + prev)
        for ts in range(TSG):
            sp_reader[ts] = rq
        m0 = ir.op("act", lambda e: e.activation(
            out=stat_mean[:], in_=c1[:], func=AF.Identity, scale=1.0 / D),
            deps=[rs] + prev)
        m1 = ir.op("act", lambda e: e.activation(
            out=c3[:], in_=c2[:], func=AF.Identity, scale=1.0 / D), deps=[rq])
        v0 = ir.op("dve", lambda e: e.tensor_tensor(
            out=c4[:], in0=stat_mean[:], in1=stat_mean[:], op=ALU.mult), deps=[m0])
        v1 = ir.op("dve", lambda e: e.tensor_tensor(
            out=c3[:], in0=c3[:], in1=c4[:], op=ALU.subtract), deps=[m1, v0])
        v2 = ir.op("dve", lambda e: e.tensor_scalar_add(out=c4[:], in0=c3[:], scalar1=EPS),
                   deps=[v1])
        l0 = ir.op("act", lambda e: e.activation(
            out=c3[:], in_=c3[:], func=AF.Ln, bias=eps_sb[:]), deps=[v1, v2, op_eps])
        y0 = ir.op("act", lambda e: e.activation(
            out=c3[:], in_=c3[:], func=AF.Exp, scale=-0.5), deps=[l0])
        n0 = ir.op("dve", lambda e: e.tensor_tensor(out=c1[:], in0=c3[:], in1=c3[:], op=ALU.mult),
                   deps=[y0, rs, m0])
        n1 = ir.op("dve", lambda e: e.tensor_tensor(out=c1[:], in0=c1[:], in1=c4[:], op=ALU.mult),
                   deps=[n0, v2])
        n2 = ir.op("act", lambda e: e.activation(out=c1[:], in_=c1[:], func=AF.Identity,
                                                 scale=-0.5, bias=onep5_sb[:]),
                   deps=[n1, op_1p5])
        n3 = ir.op("dve", lambda e: e.tensor_tensor(
            out=stat_rstd[:], in0=c3[:], in1=c1[:], op=ALU.mult), deps=[n2, y0])
        b0 = ir.op("dve", lambda e: e.tensor_tensor(
            out=c4[:], in0=stat_mean[:], in1=stat_rstd[:], op=ALU.mult), deps=[n3, n1])
        b1 = ir.op("act", lambda e: e.activation(
            out=stat_mb[:], in_=c4[:], func=AF.Identity, scale=-1.0), deps=[b0, n2])
        chain_tail[0], chain_tail[1] = b0, b1
        return [n3, b1]

    out_wr = []
    for tg in range(NTG):
        attnT, g_attnT = sb(f"attnT{tg}", [128, H * TG], BF16)
        pscg = []
        psc = []
        for i in range(TSG):
            t, g = ps(f"psc{tg}_{i}", [128, 512]); pscg.append(g); psc.append(t)
        atld = []
        W2 = H * 512 // 2
        for qh in range(TG // 512):
            qt = tg * (TG // 512) + qh
            for hv in range(2):
                atld.append(ir.op("sp", lambda e, qt=qt, qh=qh, hv=hv: e.dma_start(
                    out=attnT[:, qh * H * 512 + hv * W2:qh * H * 512 + (hv + 1) * W2],
                    in_=a_dram[qt, :, hv * W2:(hv + 1) * W2]),
                    deps=slab_wr_q[qt] + [last_ffn_mm[0]],
                    dma=True, chan=f"atld{len(atld)}"))

        for oc in range(NOC):
            stop_ops = {}
            xl_ops = {}
            for ts in range(TSG):
                i = xi[0] + ts; slot = i % 8
                rows = tg * TG + ts * 128
                xl_ops[ts] = ir.op("act", lambda e, rows=rows, oc=oc, slot=slot: e.dma_start(
                    out=xres[slot][:], in_=x_q[rows:rows + 128, oc * 512:(oc + 1) * 512]),
                    deps=[xres_user.get(slot)], dma=True, chan=f"xr{slot}")
            for ts in range(TSG):
                bmm = pe_op(lambda e, ts=ts, oc=oc: e.matmul(
                    psc[ts][:], orb_sb[:], bo_bc[0:1, oc * 512:(oc + 1) * 512],
                    start=True, stop=False),
                    [d_orb, bank_prev.get(ts), atld[(ts // 4) * 2]] + bias_bc_done)
            for f4 in range(DC // 4):
                cst, wi = wstream4(wo, oc, f4)
                lastmm = None
                for fi in range(4):
                    f = f4 * 4 + fi
                    for ts in range(TSG):
                        deps = [cst, atld[(ts // 4) * 2 + f // (DC // 2)]]
                        acol = (ts // 4) * H * 512 + f * 512 + (ts % 4) * 128
                        mm = pe_op(lambda e, f=f, fi=fi, ts=ts, wi=wi, acol=acol: e.matmul(
                            psc[ts][:], attnT[:, acol:acol + 128],
                            w_bf2[wi % 2][:, fi * 512:(fi + 1) * 512],
                            start=False, stop=(f == DC - 1)), deps)
                        lastmm = mm
                        if f == DC - 1:
                            stop_ops[ts] = mm
                wpe_last[wi] = lastmm
            for ts in range(TSG):
                i = xi[0]; slot = i % 8
                rows = tg * TG + ts * 128
                ev = ir.op("dve", lambda e, ts=ts, slot=slot: e.tensor_tensor(
                    out=rst[slot][:], in0=psc[ts][:], in1=xres[slot][:], op=ALU.add),
                    deps=[stop_ops[ts], xl_ops[ts]] + (rst_user.get(slot) or []))
                bank_prev[ts] = ev
                xres_user[slot] = ev
                wr = ir.op("act", lambda e, rows=rows, oc=oc, slot=slot: e.dma_start(
                    out=r1_dram[rows:rows + 128, oc * 512:(oc + 1) * 512], in_=rst[slot][:]),
                    deps=[ev], dma=True, chan=f"rs{slot}")
                r1w[(oc, ts)] = wr
                rsp = ir.op("dve", lambda e, ts=ts, oc=oc, slot=slot: e.tensor_reduce(
                    out=spart[:, ts * NOC + oc:ts * NOC + oc + 1], in_=rst[slot][:],
                    axis=AX.X, op=ALU.add), deps=[ev, sp_reader.get(ts)])
                rqp = ir.op("act", lambda e, ts=ts, oc=oc, slot=slot: e.activation(
                    out=scr_bf[:, 0:512], in_=rst[slot][:], func=AF.Square,
                    accum_out=qpart[:, ts * NOC + oc:ts * NOC + oc + 1]),
                    deps=[ev, scr_waw[0]], )
                scr_waw[0] = rqp
                rst_user[slot] = [wr, rsp, rqp]
                if oc == NOC - 1:
                    pl_done[ts] = [rsp, rqp]
                xi[0] += 1
        r1_done = [(rst_user.get(s2) or [None])[0] for s2 in range(8)]
        freeall(pscg)
        g_attnT.__exit__(None, None, None)

        x1f, g_x1f = sb(f"x1f{tg}", [128, DC * TG], BF16)
        tp2g = []
        tp2 = []
        for i in range(2):
            t, g = ps(f"tp2{tg}_{i}", [128, 128], BF16); tp2g.append(g); tp2.append(t)
        x1row, g_x1row = sb(f"x1row{tg}", [128, D], BF16)
        x1f_ev = {}
        x1row_waw = [None]
        ln1_fin = ln_stats_all([d for v in pl_done.values() for d in v])
        ln1_done = {ts: ln1_fin for ts in range(TSG)}
        for ts in range(TSG):
            slot = ts % 2
            rows = tg * TG + ts * 128
            ld = ir.op("sp", lambda e, rows=rows, slot=slot: e.dma_start(
                out=rowbuf[slot][:], in_=r1_dram[rows:rows + 128, :]),
                deps=[r1w[(oc2, ts)] for oc2 in range(NOC)] + rowbuf_user.get(slot, []),
                dma=True, chan=f"row{slot}")
            fin = ln1_fin
            xb = ir.op("dve", lambda e, ts=ts, slot=slot: e.tensor_scalar(
                out=x1row[:], in0=rowbuf[slot][:], scalar1=stat_rstd[:, ts:ts + 1],
                scalar2=stat_mb[:, ts:ts + 1], op0=ALU.mult, op1=ALU.add),
                deps=fin + [ld, x1row_waw[0]])
            rowbuf_user[slot] = [xb]
            lasttr = None
            for Dc in range(DC):
                i2 = ts * DC + Dc
                tr = pe_op(lambda e, Dc=Dc, i2=i2: e.transpose(
                    tp2[i2 % 2][:], x1row[:, Dc * 128:(Dc + 1) * 128], id_sb[:]),
                    [xb, d_id, x1f_ev.get(i2 - 2)])
                evv = ir.op("dve", lambda e, Dc=Dc, ts=ts, i2=i2: e.tensor_copy(
                    out=x1f[:, Dc * TG + ts * 128:Dc * TG + (ts + 1) * 128], in_=tp2[i2 % 2][:]),
                    deps=[tr])
                x1f_ev[i2] = evv
                lasttr = tr
            x1row_waw[0] = lasttr
        x1f_done = x1f_ev[TSG * DC - 1]
        psh = []
        pshg = []
        for i in range(2):
            t, g = ps(f"psh{tg}_{i}", [128, 512]); pshg.append(g); psh.append(t)
        h_ev = {}
        for tq in range(TG // 512):
            stop = None
            last_ts = (tq + 1) * 4 - 1
            for Dc in range(DC):
                mm = pe_op(lambda e, Dc=Dc, tq=tq: e.matmul(
                    psh[tq % 2][:], at_bf[:, Dc * 128:(Dc + 1) * 128],
                    x1f[:, Dc * TG + tq * 512:Dc * TG + tq * 512 + 512],
                    start=(Dc == 0), stop=(Dc == DC - 1)),
                    [x1f_ev[last_ts * DC + DC - 1], atc, h_ev.get(tq - 2)])
                stop = mm
            hv = ir.op("act", lambda e, tq=tq: e.activation(
                out=hT_sb[:, tq * 512:tq * 512 + 512], in_=psh[tq % 2][:],
                func=AF.Identity, scale=coef[:]), deps=[stop, op_coef])
            h_ev[tq] = hv
        hT_done = [h_ev[tq] for tq in range(TG // 512)]
        freeall(pshg)
        freeall(tp2g)
        g_x1row.__exit__(None, None, None)

        pscg = []
        psc = []
        for i in range(TSG):
            t, g = ps(f"psd{tg}_{i}", [128, 512]); pscg.append(g); psc.append(t)
        for oc in range(NOC):
            stop_ops = {}
            xl_ops = {}
            for ts in range(TSG):
                i = xi[0] + ts; slot = i % 8
                rows = tg * TG + ts * 128
                rl = ir.op("sp", lambda e, rows=rows, oc=oc, slot=slot: e.dma_start(
                    out=xres[slot][:], in_=r1_dram[rows:rows + 128, oc * 512:(oc + 1) * 512]),
                    deps=[r1w[(oc, ts)], xres_user.get(slot)], dma=True, chan=f"xr{slot}")
                xc = ir.op("act", lambda e, ts=ts, slot=slot: e.activation(
                    out=xres[slot][:], in_=xres[slot][:], func=AF.Identity,
                    scale=stat_rstd[:, ts:ts + 1], bias=stat_mb[:, ts:ts + 1]),
                    deps=[rl] + ln1_done[ts])
                c2_xc_last[0] = xc
                xl_ops[ts] = xc
            for ts in range(TSG):
                bmm = pe_op(lambda e, ts=ts, oc=oc: e.matmul(
                    psc[ts][:], orb_sb[:], bff_bc[0:1, oc * 512:(oc + 1) * 512],
                    start=True, stop=False),
                    [d_orb, bank_prev.get(ts)] + bias_bc_done + (hT_done if oc == 0 else []))
            for f4 in range(DC // 4):
                cst, wi = wstream4(wff, oc, f4)
                lastmm = None
                for fi in range(4):
                    f = f4 * 4 + fi
                    for ts in range(TSG):
                        deps = [cst, x1f_ev[ts * DC + f]]
                        mm = pe_op(lambda e, f=f, fi=fi, ts=ts, wi=wi: e.matmul(
                            psc[ts][:], x1f[:, f * TG + ts * 128:f * TG + (ts + 1) * 128],
                            w_bf2[wi % 2][:, fi * 512:(fi + 1) * 512],
                            start=False, stop=False), deps)
                        lastmm = mm
                wpe_last[wi] = lastmm
            for ts in range(TSG):
                lmm = pe_op(lambda e, ts=ts, oc=oc: e.matmul(
                    psc[ts][:], hT_sb[:, ts * 128:(ts + 1) * 128],
                    bt_bf[:, oc * 512:(oc + 1) * 512], start=False, stop=True),
                    hT_done + [btc])
                stop_ops[ts] = lmm
                last_ffn_mm[0] = lmm
            for ts in range(TSG):
                i = xi[0]; slot = i % 8
                rows = tg * TG + ts * 128
                ev = ir.op("dve", lambda e, ts=ts, slot=slot: e.tensor_tensor(
                    out=rst[slot][:], in0=psc[ts][:], in1=xres[slot][:], op=ALU.add),
                    deps=[stop_ops[ts], xl_ops[ts]] + (rst_user.get(slot) or []))
                bank_prev[ts] = ev
                xres_user[slot] = ev
                wr = ir.op("act", lambda e, rows=rows, oc=oc, slot=slot: e.dma_start(
                    out=r2_dram[rows:rows + 128, oc * 512:(oc + 1) * 512], in_=rst[slot][:]),
                    deps=[ev], dma=True, chan=f"rs{slot}")
                r2w[(oc, ts)] = wr
                rsp = ir.op("dve", lambda e, ts=ts, oc=oc, slot=slot: e.tensor_reduce(
                    out=spart[:, ts * NOC + oc:ts * NOC + oc + 1], in_=rst[slot][:],
                    axis=AX.X, op=ALU.add), deps=[ev, sp_reader.get(ts)])
                rqp = ir.op("act", lambda e, ts=ts, oc=oc, slot=slot: e.activation(
                    out=scr_bf[:, 0:512], in_=rst[slot][:], func=AF.Square,
                    accum_out=qpart[:, ts * NOC + oc:ts * NOC + oc + 1]),
                    deps=[ev, scr_waw[0]], )
                scr_waw[0] = rqp
                rst_user[slot] = [wr, rsp, rqp]
                if oc == NOC - 1:
                    pl_done[ts] = [rsp, rqp]
                xi[0] += 1
        r2_done = [(rst_user.get(s2) or [None])[0] for s2 in range(8)]
        freeall(pscg)
        g_x1f.__exit__(None, None, None)

        ln2_fin = ln_stats_all([d for v in pl_done.values() for d in v] + [c2_xc_last[0]])
        D2 = D // 2
        for ts in range(TSG):
            slot = ts % 2
            rows = tg * TG + ts * 128
            lds = []
            ldeps = [r2w[(oc2, ts)] for oc2 in range(NOC)]
            for hv in range(2):
                lds.append(ir.op("act", lambda e, rows=rows, slot=slot, hv=hv: e.dma_start(
                    out=rowbuf[slot][:, hv * D2:(hv + 1) * D2],
                    in_=r2_dram[rows:rows + 128, hv * D2:(hv + 1) * D2]),
                    deps=ldeps + [c2_xc_last[0]] + rowbuf_user.get(slot, []),
                    dma=True, chan=f"row{slot}" if hv == 0 else f"row{slot}b"))
            fin = ln2_fin
            oa2 = []
            D4 = D // 4
            for qv in range(4):
                osl = (ts * 4 + qv) % 5
                oa = ir.op("dve", lambda e, ts=ts, slot=slot, qv=qv, osl=osl, D4=D4: e.tensor_scalar(
                    out=outst[osl][:], in0=rowbuf[slot][:, qv * D4:(qv + 1) * D4],
                    scalar1=stat_rstd[:, ts:ts + 1],
                    scalar2=stat_mb[:, ts:ts + 1], op0=ALU.mult, op1=ALU.add),
                    deps=fin + [lds[qv // 2]] + (outlast.get(osl) or []))
                oa2.append(oa)
                ow = ir.op("act", lambda e, rows=rows, qv=qv, osl=osl, D4=D4: e.dma_start(
                    out=y[rows:rows + 128, qv * D4:(qv + 1) * D4],
                    in_=outst[osl][:]),
                    deps=[oa], dma=True, chan=f"out{osl}")
                outlast[osl] = [ow]
                out_wr.append(ow)
            rowbuf_user[slot] = oa2 + fin

    ir.op("sp", None, deps=out_wr[-5:])

    ir.resolve(nc, stack)
    ir.emit(nc)
    stack.close()
    freeall(gc)
    freeall(guards)
    nc.finalize()
    return nc



def make_core_inputs(cfg, x, wq, bq, wk, bk, wv, bv, wo, bo, la, lb, wff, bff, lwts, b, half):
    T, D = cfg["T"], cfg["D"]
    blocks = half_blocks(T, half)
    idx = np.concatenate([np.arange(bb * 128, (bb + 1) * 128) for bb in blocks])
    xb = np.asarray(x[b], np.float32)
    xt = np.ascontiguousarray(xb.T)
    NR = int(np.asarray(la).shape[0]) * int(np.asarray(la).shape[1])
    d = dict(
        xt_kv=xt,
        xt_q=np.ascontiguousarray(xt[:, idx]),
        x_q=np.ascontiguousarray(xb[idx]),
        wq=np.asarray(wq, np.float32), wk=np.asarray(wk, np.float32),
        wv=np.asarray(wv, np.float32), wo=np.asarray(wo, np.float32),
        wff=np.ascontiguousarray(np.asarray(wff, np.float32).T),
        at=np.ascontiguousarray(np.asarray(la, np.float32).transpose(2, 0, 1).reshape(D, NR)),
        bt=np.ascontiguousarray(np.asarray(lb, np.float32).transpose(0, 2, 1).reshape(NR, D)),
        bq=np.asarray(bq, np.float32), bk=np.asarray(bk, np.float32),
        bv=np.asarray(bv, np.float32), bo=np.asarray(bo, np.float32),
        bff=np.asarray(bff, np.float32),
        lw=np.repeat(np.asarray(lwts, np.float32), int(np.asarray(la).shape[1])),
        masks=build_masks(cfg, half),
        ident=np.eye(128, dtype=BF16NP),
        onescol=np.ones((128, 1), BF16NP),
        onesrowf=np.ones((1, 128), np.float32),
        onesrowb=np.ones((1, 128), BF16NP),
    )
    return d, idx


def kernel(L0_attn_x, L0_attn_Wq, L0_attn_bq, L0_attn_Wk, L0_attn_bk,
           L0_attn_Wv, L0_attn_bv, L0_attn_Wo, L0_attn_bo,
           L0_ffn_lora_A, L0_ffn_lora_B, L0_ffn_W, L0_ffn_bias, L0_ffn_lora_weights):
    cfg = real_cfg()
    B = 4
    nc = build_program(cfg)
    in_maps = []
    idxs = []
    for c in range(8):
        b, half = c // 2, c % 2
        d, idx = make_core_inputs(cfg, L0_attn_x, L0_attn_Wq, L0_attn_bq,
                                  L0_attn_Wk, L0_attn_bk, L0_attn_Wv, L0_attn_bv,
                                  L0_attn_Wo, L0_attn_bo, L0_ffn_lora_A, L0_ffn_lora_B,
                                  L0_ffn_W, L0_ffn_bias, L0_ffn_lora_weights, b, half)
        in_maps.append(d)
        idxs.append((b, idx))
    res = run_bass_kernel_spmd(nc, in_maps, core_ids=list(range(8)))
    out = np.empty((B, cfg["T"], cfg["D"]), np.float32)
    for c in range(8):
        b, idx = idxs[c]
        out[b, idx] = np.asarray(res.results[c]["y"], np.float32)
    return out


# version 31
# speedup vs baseline: 1.1735x; 1.0047x over previous
import numpy as np
import ml_dtypes
from contextlib import ExitStack

import concourse.bass as bass
import concourse.mybir as mybir
from concourse.bass_utils import run_bass_kernel_spmd

F32 = mybir.dt.float32
BF16 = mybir.dt.bfloat16
AF = mybir.ActivationFunctionType
ALU = mybir.AluOpType
AX = mybir.AxisListType
BF16NP = ml_dtypes.bfloat16


def real_cfg():
    return dict(T=4096, D=4096, H=32, KVH=8, NQ=2048, GT=2048, TG=1024)


def half_blocks(T, half):
    nw = T // 1024
    if half == 0:
        return [8 * j + d for j in range(nw) for d in (0, 3, 4, 7)]
    return [8 * j + d for j in range(nw) for d in (1, 2, 5, 6)]


def build_masks(cfg, half):
    T, NQ = cfg["T"], cfg["NQ"]
    NQT = NQ // 512
    blocks = half_blocks(T, half)
    m = np.zeros((NQT, 8, 128, 128), np.float32)
    for j in range(NQT):
        gq = np.concatenate([np.arange(b * 128, (b + 1) * 128) for b in blocks[4 * j:4 * j + 4]])
        for c in range(8):
            kk = (8 * j + c) * 128 + np.arange(128)
            qq = gq[(c // 2) * 128:(c // 2) * 128 + 128]
            m[j, c] = (kk[:, None] <= qq[None, :]).astype(np.float32)
    return m.astype(BF16NP)



class IR:

    def __init__(self):
        self.ops = []
        self.phases = []
        self._cur = None
        self._last = {}
        self._barrier = []

    def phase(self, name):
        self._barrier = sorted(self._last.values())
        self._cur = []
        self.phases.append((name, self._cur))

    def op(self, eng, fn, deps=(), dma=False, chan=None):
        i = len(self.ops)
        assert (chan is not None) == dma, "dma ops need a chan; compute ops must not"
        self.ops.append(dict(eng=eng, fn=fn, dma=dma, chan=chan,
                             deps=sorted({d for d in deps if d is not None} | set(self._barrier)),
                             sig=False, sem=None, val=None))
        self._cur.append(i)
        self._last[chan if dma else eng] = i
        return i

    def resolve(self, nc, ctx):
        for o in self.ops:
            for d in o["deps"]:
                self.ops[d]["sig"] = True
        sems, counts = {}, {}
        for pi, (pname, opids) in enumerate(self.phases):
            for i in opids:
                o = self.ops[i]
                if not (o["sig"] or o["dma"]):
                    continue
                if o["dma"]:
                    key = ("d", o["chan"])
                else:
                    key = ("c", o["eng"], pi)
                if key not in sems:
                    sems[key] = ctx.enter_context(nc.semaphore("_".join(str(k) for k in key)))
                    counts[key] = 0
                counts[key] += 16 if o["dma"] else 1
                o["sem"], o["val"] = sems[key], counts[key]

    def emit(self, nc):
        engmap = dict(sp="sync", act="scalar", dve="vector", pe="tensor", pool="gpsimd")
        for pi, (pname, opids) in enumerate(self.phases):
            byeng = {}
            for i in opids:
                byeng.setdefault(self.ops[i]["eng"], []).append(i)
            with nc.Block(pname) as block:
                for eng, ids in byeng.items():
                    def body(e, ids=ids):
                        waited = {}
                        for i in ids:
                            o = self.ops[i]
                            for d in o["deps"]:
                                p = self.ops[d]
                                if waited.get(p["sem"].num, -1) < p["val"]:
                                    e.wait_ge(p["sem"], p["val"])
                                    waited[p["sem"].num] = p["val"]
                            if o["fn"] is not None:
                                inst = o["fn"](e)
                                if o["sig"] or o["dma"]:
                                    inst.then_inc(o["sem"], 16 if o["dma"] else 1)
                    getattr(block, engmap[eng])(body)



def build_program(cfg):
    T, D, H, KVH = cfg["T"], cfg["D"], cfg["H"], cfg["KVH"]
    NQ, GT, TG = cfg["NQ"], cfg["GT"], cfg["TG"]
    DC = D // 128
    DKV = KVH * 128
    REP = H // KVH
    NG = T // GT
    NSUB = GT // 512
    NSUBQ = NQ // 512
    NQT = NQ // 512
    NTG = NQ // TG
    TSG = TG // 128
    NOC = D // 512
    NKC = T // 128
    ISQ = 1.0 / float(np.sqrt(128.0))
    EPS = 1e-5
    SCALE = 2.0

    nc = bass.Bass("TRN2", target_bir_lowering=False, debug=False)

    def din(name, shape, dt=F32):
        return nc.dram_tensor(name, shape, dt, kind="ExternalInput").ap()

    xt_kv = din("xt_kv", [D, T]); xt_q = din("xt_q", [D, NQ]); x_q = din("x_q", [NQ, D])
    wq = din("wq", [D, D]); wk = din("wk", [D, DKV]); wv = din("wv", [D, DKV])
    wo = din("wo", [D, D]); wff = din("wff", [D, D])
    at = din("at", [D, 128]); bt = din("bt", [128, D])
    bq = din("bq", [D]); bk = din("bk", [DKV]); bv = din("bv", [DKV])
    bo = din("bo", [D]); bff = din("bff", [D]); lw = din("lw", [128])
    masks = din("masks", [NQT, 8, 128, 128], BF16)
    ident = din("ident", [128, 128], BF16)
    onescol = din("onescol", [128, 1], BF16)
    onesrowf = din("onesrowf", [1, 128])
    onesrowb = din("onesrowb", [1, 128], BF16)
    y = nc.dram_tensor("y", [NQ, D], F32, kind="ExternalOutput").ap()

    k_dram = nc.dram_tensor("k_scr", [KVH, 128, T], BF16, kind="Internal").ap()
    v_dram = nc.dram_tensor("v_scr", [KVH, 128, T], BF16, kind="Internal").ap()
    q_dram = nc.dram_tensor("q_scr", [H, 128, NQ], BF16, kind="Internal").ap()
    a_dram = nc.dram_tensor("a_scr", [NQT, 128, H * 512], BF16, kind="Internal").ap()
    r1_dram = nc.dram_tensor("r1_scr", [NQ, D], F32, kind="Internal").ap()
    r2_dram = nc.dram_tensor("r2_scr", [NQ, D], F32, kind="Internal").ap()

    ir = IR()
    stack = ExitStack()
    guards = []

    def sb(name, shape, dt):
        g = nc.sbuf_tensor(name, shape, dt)
        return g.__enter__(), g

    def ps(name, shape, dt=F32):
        g = nc.psum_tensor(name, shape, dt)
        return g.__enter__(), g

    def freeall(gs):
        for g in reversed(gs):
            g.__exit__(None, None, None)

    def sbp(name, shape, dt):
        t, g = sb(name, shape, dt)
        guards.append(g)
        return t

    bk_sb = sbp("bk_sb", [128, KVH], F32)
    bv_sb = sbp("bv_sb", [128, KVH], F32)
    bq_sb = sbp("bq_sb", [128, H], F32)
    lw_sb = sbp("lw_sb", [128, 1], F32)
    coef = sbp("coef", [128, 1], F32)
    eps_sb = sbp("eps_sb", [128, 1], F32)
    onep5_sb = sbp("onep5_sb", [128, 1], F32)
    id_sb = sbp("id_sb", [128, 128], BF16)
    ocol_sb = sbp("ocol_sb", [128, 1], BF16)
    orf_sb = sbp("orf_sb", [1, 128], F32)
    orb_sb = sbp("orb_sb", [1, 128], BF16)

    ir.phase("setup")
    NCslow = dict(allow_slow_non_contiguous=True)
    d_bk = ir.op("sp", lambda e: e.dma_start(out=bk_sb[:], in_=bk.rearrange("(i p) -> p i", p=128), **NCslow), dma=True, chan="st")
    d_bv = ir.op("sp", lambda e: e.dma_start(out=bv_sb[:], in_=bv.rearrange("(i p) -> p i", p=128), **NCslow), deps=[d_bk], dma=True, chan="st")
    d_bq = ir.op("sp", lambda e: e.dma_start(out=bq_sb[:], in_=bq.rearrange("(i p) -> p i", p=128), **NCslow), deps=[d_bv], dma=True, chan="st")
    d_lw = ir.op("sp", lambda e: e.dma_start(out=lw_sb[:], in_=lw.rearrange("(p o) -> p o", o=1), **NCslow), deps=[d_bq], dma=True, chan="st")
    d_id = ir.op("sp", lambda e: e.dma_start(out=id_sb[:], in_=ident[:]), deps=[d_lw], dma=True, chan="st")
    d_oc = ir.op("sp", lambda e: e.dma_start(out=ocol_sb[:], in_=onescol[:]), deps=[d_id], dma=True, chan="st")
    d_orf = ir.op("sp", lambda e: e.dma_start(out=orf_sb[:], in_=onesrowf[:]), deps=[d_oc], dma=True, chan="st")
    d_orb = ir.op("sp", lambda e: e.dma_start(out=orb_sb[:], in_=onesrowb[:]), deps=[d_orf], dma=True, chan="st")
    biasdeps = [d_bq]
    op_coef = ir.op("act", lambda e: e.activation(out=coef[:], in_=lw_sb[:], func=AF.Identity, scale=SCALE), deps=[d_lw])
    op_eps = ir.op("dve", lambda e: e.memset(eps_sb[:], EPS))
    op_1p5 = ir.op("dve", lambda e: e.memset(onep5_sb[:], 1.5))

    ir.phase("qkv")
    ga = []
    GX = max(GT, NQ)
    x_bf, g = sb("x_bf", [128, DC * GX], BF16); ga.append(g)
    w_bf, g = sb("w_bf", [128, DC * 512], BF16); ga.append(g)
    xstage = []
    for i in range(2):
        t, g = sb(f"xstage{i}", [128, GX], F32); ga.append(g); xstage.append(t)
    estage = []
    for i in range(4):
        t, g = sb(f"estage{i}", [128, 512], BF16); ga.append(g); estage.append(t)
    pa = []
    pag = []
    for i in range(8):
        t, g = ps(f"pa{i}", [128, 512]); pag.append(g); pa.append(t)

    kv_chunks = ([("k", wk, i, bk_sb, i, k_dram) for i in range(KVH)] +
                 [("v", wv, i, bv_sb, i, v_dram) for i in range(KVH)])
    q_chunks = [("q", wq, i, bq_sb, i, q_dram) for i in range(H)]

    last_wr = {"k": None, "v": None, "q": None}
    bank_prev = {}
    est_wr = {}
    ws_cast = {}
    wpe_last = {}
    xld_prev = [None, None, None, None]
    wld_prev = [None, None]
    wa_use = {}
    w_loaded = [None]
    w_saved = [None]
    pe_last_half = {0: None, 1: None}
    pe_last_half_cur = {0: None, 1: None}
    ev_i, ws_i = [0], [0]
    last_pe_op = [None]

    def pe_op(fn, deps):
        i = ir.op("pe", fn, deps)
        last_pe_op[0] = i
        return i

    x_bf3 = x_bf[:].rearrange("p (c t) -> p c t", c=DC)
    w_bf3 = w_bf[:].rearrange("p (c n) -> p c n", c=DC)

    def emit_wload(wten, col0, width, Dc4, wc_ops):
        dset = list(range(Dc4 * 4, Dc4 * 4 + 4))
        ld = ir.op("pool", lambda e, wten=wten, Dc4=Dc4, col0=col0, width=width: e.dma_start(
            out=w_bf3[:, Dc4 * 4:(Dc4 + 1) * 4, 0:width],
            in_=wten[Dc4 * 512:(Dc4 + 1) * 512, col0:col0 + width].rearrange(
                "(c p) n -> p c n", p=128)),
            deps=[wld_prev[Dc4 % 2]] + [wa_use.get(d) for d in dset],
            dma=True, chan=f"wa{Dc4 % 2}")
        wld_prev[Dc4 % 2] = ld
        for d in dset:
            wc_ops[d] = ld

    def gen_gemm(src, gcount, gt, nsub, chunks, oc_base, same_layout=False,
                 startup=False):
        halves = 2 if nsub >= 2 else 1
        gth = gt // halves
        nsh = nsub // halves
        x3loc = x_bf[:].rearrange("p (c t) -> p c t", c=DC) if gt == GX else \
            x_bf[:, 0:DC * gt].rearrange("p (c t) -> p c t", c=DC)
        for gi in range(gcount):
            slabs = []
            for kind in ("k", "v", "q"):
                kc_ = [ch for ch in chunks if ch[0] == kind]
                slabs += [kc_[i:i + 4] for i in range(0, len(kc_), 4)]
            if gi % 2 == 1:
                slabs = slabs[::-1]
            inter_w = startup and gi == 0
            if inter_w:
                s0 = slabs[0]
                w0ten, w0col = s0[0][1], s0[0][2] * 128
                w0width = len(s0) * 128
                w0key = (s0[0][0], s0[0][2], w0width, id(w0ten))
                pre_ops = {}
                w_dc4 = [0]
            xc_ops = {}
            for half in range(halves):
                hdeps = [pe_last_half[half]] if (gi > 0 or same_layout) \
                    else [pe_last_half[0], pe_last_half[1]]
                for Dc2 in range(DC // 2):
                    Dc = 2 * Dc2
                    ld = ir.op("pool", lambda e, Dc=Dc, gi=gi, half=half, src=src, gt=gt, gth=gth, x3loc=x3loc: e.dma_start(
                        out=x3loc[:, Dc:Dc + 2, half * gth:(half + 1) * gth],
                        in_=src[Dc * 128:(Dc + 2) * 128,
                                gi * gt + half * gth:gi * gt + (half + 1) * gth].rearrange(
                                    "(c p) t -> p c t", p=128)),
                        deps=[xld_prev[Dc2 % 4]] + hdeps, dma=True, chan=f"xa{Dc2 % 4}")
                    xld_prev[Dc2 % 4] = ld
                    xc_ops[(half, Dc)] = ld
                    xc_ops[(half, Dc + 1)] = ld
                    if inter_w and half == 0 and Dc2 % 2 == 1 and w_dc4[0] < DC // 4:
                        emit_wload(w0ten, w0col, w0width, w_dc4[0], pre_ops)
                        w_dc4[0] += 1
            if inter_w:
                while w_dc4[0] < DC // 4:
                    emit_wload(w0ten, w0col, w0width, w_dc4[0], pre_ops)
                    w_dc4[0] += 1
                w_loaded[0] = w0key
                w_saved[0] = pre_ops
            for slab in slabs:
                width = len(slab) * 128
                wten, col0 = slab[0][1], slab[0][2] * 128
                skey = (slab[0][0], slab[0][2], width, id(wten))
                if w_loaded[0] == skey:
                    wc_ops = w_saved[0]
                else:
                    wc_ops = {}
                    for Dc4 in range(DC // 4):
                        emit_wload(wten, col0, width, Dc4, wc_ops)
                new_lastuse = {}
                is_first_after = (slab is slabs[0]) and (gi > 0 or same_layout)
                if (slab is slabs[-1]) or is_first_after:
                    pairs = [(ci, half) for half in range(halves) for ci in range(len(slab))]
                else:
                    pairs = [(ci, half) for ci in range(len(slab)) for half in range(halves)]
                for ci, half in pairs:
                    kind, _w, chunk_i, bias_sb, bias_i, dest = slab[ci]
                    par = ci % 2
                    stop_ops = {}
                    for Dc in range(DC):
                        for sub in range(nsh):
                            bank = par * 4 + half * nsh + sub
                            deps = [wc_ops[Dc], xc_ops[(half, Dc)]]
                            if Dc == 0:
                                deps.append(bank_prev.get(bank))
                            mm = pe_op(lambda e, bank=bank, Dc=Dc, ci=ci, half=half, sub=sub, gt=gt, gth=gth: e.matmul(
                                pa[bank][:],
                                w_bf[:, Dc * 512 + ci * 128:Dc * 512 + (ci + 1) * 128],
                                x_bf[:, Dc * gt + half * gth + sub * 512:Dc * gt + half * gth + sub * 512 + 512],
                                start=(Dc == 0), stop=(Dc == DC - 1)), deps)
                            if Dc == DC - 1:
                                stop_ops[sub] = mm
                            pe_last_half_cur[half] = mm
                            new_lastuse[Dc] = mm
                    for sub in range(nsh):
                        bank = par * 4 + half * nsh + sub
                        i = ev_i[0]; slot = i % 4
                        act = ir.op("act", lambda e, bank=bank, slot=slot, bias_sb=bias_sb, bias_i=bias_i: e.activation(
                            out=estage[slot][:], in_=pa[bank][:], func=AF.Identity,
                            bias=bias_sb[:, bias_i:bias_i + 1]),
                            deps=[stop_ops[sub], est_wr.get(i - 4)] + biasdeps)
                        bank_prev[bank] = act
                        if kind == "q":
                            c0 = half * gth + sub * 512
                        else:
                            c0 = gi * gt + half * gth + sub * 512
                        wr = ir.op("act", lambda e, dest=dest, chunk_i=chunk_i, c0=c0, slot=slot: e.dma_start(
                            out=dest[chunk_i, :, c0:c0 + 512], in_=estage[slot][:]),
                            deps=[act], dma=True, chan=f"es{slot}")
                        est_wr[i] = wr
                        last_wr[kind] = wr
                        ev_i[0] += 1
                for d in range(DC):
                    wa_use[d] = new_lastuse[d]
                w_loaded[0] = skey
                w_saved[0] = wc_ops
            pe_last_half[0] = pe_last_half_cur[0]
            pe_last_half[1] = pe_last_half_cur[1]

    gen_gemm(xt_kv, NG, GT, NSUB, kv_chunks, 0, startup=True)
    gen_gemm(xt_q, 1, NQ, NSUBQ, q_chunks, len(kv_chunks),
             same_layout=(NQ == GT))
    a_done = [last_wr["k"], last_wr["v"], last_wr["q"]]
    freeall(pag)
    freeall(ga)

    ir.phase("attn")
    gb = []
    k_sb, g = sb("k_sb", [128, KVH * T], BF16); gb.append(g)
    v_sb, g = sb("v_sb", [128, NKC * DKV], BF16); gb.append(g)
    q_sb = []
    for i in range(2):
        t, g = sb(f"q_sb{i}", [128, 512], BF16); gb.append(g); q_sb.append(t)
    pt_sb = []
    for i in range(4):
        t, g = sb(f"pt_sb{i}", [128, 512], BF16); gb.append(g); pt_sb.append(t)
    mask_sb = []
    for i in range(2):
        t, g = sb(f"mask_sb{i}", [128, 8 * 128], BF16); gb.append(g); mask_sb.append(t)
    aslab, g = sb("aslab", [128, H * 512], BF16); gb.append(g)
    t, g = sb("rt_sb0", [128, 512], F32); gb.append(g)
    rt_sb = [t, t]
    t, g = sb("den_sb0", [1, 512], F32); gb.append(g)
    den_sb = [t, t]
    t, g = sb("rec_sb0", [1, 512], F32); gb.append(g)
    rec_sb = [t, t]

    kld = {}
    vtld = {}
    vt_prev = {}

    def need_k(kv):
        if kv not in kld:
            kld[kv] = ir.op("sp", lambda e, kv=kv: e.dma_start(
                out=k_sb[:, kv * T:(kv + 1) * T], in_=k_dram[kv]),
                deps=a_done + [kld.get(kv - 1)], dma=True, chan="kld")
        return kld[kv]

    def need_v(kv, kc):
        if (kv, kc) not in vtld:
            vtld[(kv, kc)] = ir.op("sp", lambda e, kv=kv, kc=kc: e.dma_start(
                out=v_sb[:, kc * DKV + kv * 128:kc * DKV + (kv + 1) * 128],
                in_=v_dram[kv, :, kc * 128:(kc + 1) * 128], transpose=True),
                deps=a_done + [vt_prev.get(kv)], dma=True, chan=f"vt{kv}")
            vt_prev[kv] = vtld[(kv, kc)]
        return vtld[(kv, kc)]

    psg = []
    def ps2(name):
        t, g = ps(name, [128, 512]); psg.append(g); return t
    ps_s = [ps2(f"ps_s{i}") for i in range(4)]
    ps_av = [ps2(f"ps_av{i}") for i in range(2)]
    t, g = ps("ps_den0", [1, 512]); psg.append(g)
    ps_den = [t, t]
    ps_rt = [ps2("ps_rt0")] * 2

    ptlast = {}
    sfree = {}
    qlast = {}
    at_evac = {}
    rt_evac = {}
    dclast = {}
    lnlast = {}
    rtmm_last = {}
    av_par = {}
    tail_prev = [None]
    swq = {}
    slab_wr_q = {}
    QD = max(H // 2, 1)
    mask_last = {}
    mld_ops = {}
    mld_prev = [None]
    b_attn_done = None

    def need_mask(qt2):
        if qt2 not in mld_ops and qt2 < NQT:
            buf = qt2 % 2
            mld_ops[qt2] = ir.op("sp", lambda e, qt2=qt2, buf=buf: e.dma_start(
                out=mask_sb[buf][:].rearrange("p (c q) -> p c q", c=8),
                in_=masks[qt2].rearrange("c p q -> p c q")),
                deps=a_done + [mask_last.get(buf), mld_prev[0]], dma=True, chan="mld")
            mld_prev[0] = mld_ops[qt2]
        return mld_ops.get(qt2)

    for qt in range(NQT):
        NK = 8 * (qt + 1)
        mld = need_mask(qt)
        need_mask(qt + 1)
        mbuf = qt % 2
        LAG = 3
        queue = []
        tails = []

        def pump():
            fn = queue.pop(0)
            fn()
            for tset in tails:
                tset[0] -= 1
            while tails and tails[0][0] <= 0:
                tails.pop(0)[1]()

        for h in range(H):
            kv = h // REP
            hp = h % 2
            ql = ir.op("sp", lambda e, h=h, qt=qt, hp=hp: e.dma_start(
                out=q_sb[hp][:], in_=q_dram[h, :, qt * 512:(qt + 1) * 512]),
                deps=[a_done[2], qlast.get(hp)], dma=True, chan=f"ql{hp}")
            exp_cur = {}
            stopv = {}

            def srange(kc, NK=NK):
                if kc >= NK - 8:
                    return 128 * ((kc - (NK - 8)) // 2)
                return 0

            def mk_avden(kc, h=h, kv=kv, hp=hp, NK=NK, exp_cur=exp_cur, stopv=stopv, srange=srange):
                def go():
                    s = srange(kc)
                    par = kc % 4
                    cur = exp_cur[kc]
                    avdeps = [cur, need_v(kv, kc)]
                    if kc == 0:
                        avdeps += [av_par.get(hp), dclast.get(0)]
                    av = pe_op(lambda e, kc=kc, kv=kv, par=par, hp=hp, NK=NK, s=s: e.matmul(
                        ps_av[hp][:, s:512], v_sb[:, kc * DKV + kv * 128:kc * DKV + (kv + 1) * 128],
                        pt_sb[par][:, s:512], start=(kc == 0), stop=(kc == NK - 1)), avdeps)
                    dn = pe_op(lambda e, kc=kc, par=par, hp=hp, NK=NK, s=s: e.matmul(
                        ps_den[hp][:, s:512], ocol_sb[:], pt_sb[par][:, s:512],
                        start=(kc == 0), stop=(kc == NK - 1)), [cur, d_oc])
                    ptlast[par] = dn
                    if kc == NK - 1:
                        stopv[0], stopv[1] = av, dn
                return go

            for kc in range(NK):
                par = kc % 4
                s = srange(kc)
                mm = pe_op(lambda e, kc=kc, kv=kv, par=par, hp=hp, s=s: e.matmul(
                    ps_s[par][:, s:512], k_sb[:, kv * T + kc * 128:kv * T + (kc + 1) * 128],
                    q_sb[hp][:, s:512], start=True, stop=True),
                    [ql, need_k(kv), sfree.get(par)])
                qlast[hp] = mm
                ex = ir.op("act", lambda e, par=par, s=s: e.activation(
                    out=pt_sb[par][:, s:512], in_=ps_s[par][:, s:512], func=AF.Exp, scale=ISQ),
                    deps=[mm, ptlast.get(par)])
                sfree[par] = ex
                cur = ex
                if kc >= NK - 8:
                    c = kc - (NK - 8)
                    cur = ir.op("dve", lambda e, par=par, c=c, s=s, mbuf=mbuf: e.tensor_tensor(
                        out=pt_sb[par][:, s:s + 128], in0=pt_sb[par][:, s:s + 128],
                        in1=mask_sb[mbuf][:, c * 128:(c + 1) * 128], op=ALU.mult),
                        deps=[ex, mld])
                    mask_last[mbuf] = cur
                exp_cur[kc] = cur
                queue.append(mk_avden(kc))
                if len(queue) > LAG:
                    pump()

            def make_tails(h=h, hp=hp, stopv=stopv):
                carry = {}

                def tail1():
                    stopDen = stopv[1]
                    dc = ir.op("dve", lambda e, hp=hp: e.tensor_copy(
                        out=den_sb[hp][:], in_=ps_den[hp][:]), deps=[stopDen, lnlast.get(0)])
                    dclast[0] = dc
                    ln1 = ir.op("act", lambda e, hp=hp: e.activation(
                        out=rec_sb[hp][:], in_=den_sb[hp][:], func=AF.Ln), deps=[dc, rtmm_last.get(0)])
                    dr = ir.op("act", lambda e, hp=hp: e.activation(
                        out=rec_sb[hp][:], in_=rec_sb[hp][:], func=AF.Exp, scale=-1.0),
                        deps=[ln1])
                    lnlast[0] = dr
                    carry["dr"] = dr

                def tail2(qt=qt):
                    stopAV = stopv[0]
                    rt = pe_op(lambda e, hp=hp: e.matmul(
                        ps_rt[hp][:], orf_sb[:], rec_sb[hp][:], start=True, stop=True),
                        [carry["dr"], d_orf, rt_evac.get(0)])
                    rtmm_last[0] = rt
                    re = ir.op("act", lambda e, hp=hp: e.activation(
                        out=rt_sb[hp][:], in_=ps_rt[hp][:], func=AF.Identity),
                        deps=[rt, at_evac.get(0)])
                    rt_evac[0] = re
                    qd = h // QD
                    ae = ir.op("dve", lambda e, h=h, hp=hp: e.tensor_tensor(
                        out=aslab[:, h * 512:(h + 1) * 512], in0=ps_av[hp][:],
                        in1=rt_sb[hp][:], op=ALU.mult), deps=[stopAV, re, swq.get(qd)])
                    av_par[hp] = ae
                    at_evac[0] = ae
                return tail1, tail2
            t1, t2 = make_tails()
            tails.append([LAG, t1])
            tails.append([LAG + 2, t2])
        while queue:
            pump()
        for tset in tails:
            tset[1]()
        tails.clear()
        sws = []
        for qd in range(2):
            w = ir.op("sp", lambda e, qt=qt, qd=qd: e.dma_start(
                out=a_dram[qt, :, qd * QD * 512:(qd + 1) * QD * 512],
                in_=aslab[:, qd * QD * 512:(qd + 1) * QD * 512]),
                deps=[av_par.get(0), av_par.get(1), swq.get(qd)],
                dma=True, chan=f"sw{qd}")
            swq[qd] = w
            sws.append(w)
        slab_wr_q[qt] = sws
        b_attn_done = sws[-1]
    freeall(psg)
    freeall(gb)

    ir.phase("post")
    gc = []
    at_bf, g = sb("at_bf", [128, DC * 128], BF16); gc.append(g)
    bt_bf, g = sb("bt_bf", [128, D], BF16); gc.append(g)
    rowbuf = []
    for i in range(2):
        t, g = sb(f"rowbuf{i}", [128, D], F32); gc.append(g); rowbuf.append(t)
    outst = []
    for i in range(5):
        t, g = sb(f"outst{i}", [128, D // 4], F32); gc.append(g); outst.append(t)
    scr_bf, g = sb("scr_bf", [128, 512], BF16); gc.append(g)
    hT_sb, g = sb("hT_sb", [128, TG], BF16); gc.append(g)
    w_bf2 = []
    for i in range(2):
        t, g = sb(f"w_bf2{i}", [128, 4 * 512], BF16); gc.append(g); w_bf2.append(t)
    xres = []
    for i in range(8):
        t, g = sb(f"xres{i}", [128, 512], F32); gc.append(g); xres.append(t)
    rst = []
    for i in range(8):
        t, g = sb(f"rst{i}", [128, 512], F32); gc.append(g); rst.append(t)
    stat_mean, g = sb("stat_mean", [128, TSG], F32); gc.append(g)
    stat_rstd, g = sb("stat_rstd", [128, TSG], F32); gc.append(g)
    stat_mb, g = sb("stat_mb", [128, TSG], F32); gc.append(g)
    spart, g = sb("spart", [128, 64], F32); gc.append(g)
    qpart, g = sb("qpart", [128, 64], F32); gc.append(g)
    c1, g = sb("c1", [128, TSG], F32); gc.append(g)
    c2, g = sb("c2", [128, TSG], F32); gc.append(g)
    c3, g = sb("c3", [128, TSG], F32); gc.append(g)
    c4, g = sb("c4", [128, TSG], F32); gc.append(g)

    bo_bc, g = sb("bo_bc", [1, D], BF16); gc.append(g)
    bff_bc, g = sb("bff_bc", [1, D], BF16); gc.append(g)
    d_bo2 = ir.op("pool", lambda e: e.dma_start(
        out=bo_bc[0:1, :], in_=bo.rearrange("(o d) -> o d", o=1)),
        deps=[b_attn_done], dma=True, chan="st2")
    d_bff2 = ir.op("pool", lambda e: e.dma_start(
        out=bff_bc[0:1, :], in_=bff.rearrange("(o d) -> o d", o=1)),
        deps=[d_bo2], dma=True, chan="st2")
    bias_bc_done = [d_bo2, d_bff2]

    atl = ir.op("sp", lambda e: e.dma_start(
        out=rowbuf[0][:].rearrange("p (c n) -> p c n", c=DC),
        in_=at.rearrange("(c p) n -> p c n", p=128)), deps=[b_attn_done], dma=True, chan="row0")
    atc = ir.op("pool", lambda e: e.tensor_copy(out=at_bf[:], in_=rowbuf[0][:]), deps=[atl])
    btl = ir.op("sp", lambda e: e.dma_start(out=rowbuf[1][:], in_=bt[:]), deps=[b_attn_done], dma=True, chan="row1")
    btc = ir.op("pool", lambda e: e.tensor_copy(out=bt_bf[:], in_=rowbuf[1][:]), deps=[btl])

    rowbuf_user = {0: [atc], 1: [btc]}
    r1w = {}
    r2w = {}
    outlast = {}
    xres_user = {}
    rst_user = {}
    x1c_user = {}
    scr_waw = [None]
    sp_reader = {}
    pl_done = {}
    chain_tail = [None, None]
    last_ffn_mm = [None]
    c2_xc_last = [None]
    xi = [0]

    wcld_prev = {}

    def wstream4(wten, oc, f4):
        i = ws_i[0]; slot = i % 2
        ld = ir.op("pool", lambda e, wten=wten, oc=oc, f4=f4, slot=slot: e.dma_start(
            out=w_bf2[slot][:].rearrange("p (c n) -> p c n", c=4),
            in_=wten[f4 * 512:(f4 + 1) * 512, oc * 512:(oc + 1) * 512].rearrange(
                "(c p) n -> p c n", p=128)),
            deps=[wcld_prev.get(slot), wpe_last.get(i - 2)], dma=True, chan=f"wc{slot}")
        wcld_prev[slot] = ld
        ws_cast[i] = ld
        ws_i[0] += 1
        return ld, i

    def ln_stats_all(pl_deps):
        prev = [p for p in chain_tail if p is not None]
        rs = ir.op("dve", lambda e: e.tensor_reduce(
            out=c1[:], in_=spart[:, 0:TSG * NOC].rearrange("p (t o) -> p t o", t=TSG),
            axis=AX.X, op=ALU.add), deps=list(pl_deps) + prev)
        rq = ir.op("dve", lambda e: e.tensor_reduce(
            out=c2[:], in_=qpart[:, 0:TSG * NOC].rearrange("p (t o) -> p t o", t=TSG),
            axis=AX.X, op=ALU.add), deps=list(pl_deps) + prev)
        for ts in range(TSG):
            sp_reader[ts] = rq
        m0 = ir.op("act", lambda e: e.activation(
            out=stat_mean[:], in_=c1[:], func=AF.Identity, scale=1.0 / D),
            deps=[rs] + prev)
        m1 = ir.op("act", lambda e: e.activation(
            out=c3[:], in_=c2[:], func=AF.Identity, scale=1.0 / D), deps=[rq])
        v0 = ir.op("dve", lambda e: e.tensor_tensor(
            out=c4[:], in0=stat_mean[:], in1=stat_mean[:], op=ALU.mult), deps=[m0])
        v1 = ir.op("dve", lambda e: e.tensor_tensor(
            out=c3[:], in0=c3[:], in1=c4[:], op=ALU.subtract), deps=[m1, v0])
        v2 = ir.op("dve", lambda e: e.tensor_scalar_add(out=c4[:], in0=c3[:], scalar1=EPS),
                   deps=[v1])
        l0 = ir.op("act", lambda e: e.activation(
            out=c3[:], in_=c3[:], func=AF.Ln, bias=eps_sb[:]), deps=[v1, v2, op_eps])
        y0 = ir.op("act", lambda e: e.activation(
            out=c3[:], in_=c3[:], func=AF.Exp, scale=-0.5), deps=[l0])
        n0 = ir.op("dve", lambda e: e.tensor_tensor(out=c1[:], in0=c3[:], in1=c3[:], op=ALU.mult),
                   deps=[y0, rs, m0])
        n1 = ir.op("dve", lambda e: e.tensor_tensor(out=c1[:], in0=c1[:], in1=c4[:], op=ALU.mult),
                   deps=[n0, v2])
        n2 = ir.op("act", lambda e: e.activation(out=c1[:], in_=c1[:], func=AF.Identity,
                                                 scale=-0.5, bias=onep5_sb[:]),
                   deps=[n1, op_1p5])
        n3 = ir.op("dve", lambda e: e.tensor_tensor(
            out=stat_rstd[:], in0=c3[:], in1=c1[:], op=ALU.mult), deps=[n2, y0])
        b0 = ir.op("dve", lambda e: e.tensor_tensor(
            out=c4[:], in0=stat_mean[:], in1=stat_rstd[:], op=ALU.mult), deps=[n3, n1])
        b1 = ir.op("act", lambda e: e.activation(
            out=stat_mb[:], in_=c4[:], func=AF.Identity, scale=-1.0), deps=[b0, n2])
        chain_tail[0], chain_tail[1] = b0, b1
        return [n3, b1]

    out_wr = []
    for tg in range(NTG):
        attnT, g_attnT = sb(f"attnT{tg}", [128, H * TG], BF16)
        pscg = []
        psc = []
        for i in range(TSG):
            t, g = ps(f"psc{tg}_{i}", [128, 512]); pscg.append(g); psc.append(t)
        atld = []
        W2 = H * 512 // 4
        for qh in range(TG // 512):
            qt = tg * (TG // 512) + qh
            for hv in range(4):
                atld.append(ir.op("sp", lambda e, qt=qt, qh=qh, hv=hv: e.dma_start(
                    out=attnT[:, qh * H * 512 + hv * W2:qh * H * 512 + (hv + 1) * W2],
                    in_=a_dram[qt, :, hv * W2:(hv + 1) * W2]),
                    deps=slab_wr_q[qt] + [last_ffn_mm[0]],
                    dma=True, chan=f"atld{len(atld)}"))

        for oc in range(NOC):
            stop_ops = {}
            xl_ops = {}
            for ts in range(TSG):
                i = xi[0] + ts; slot = i % 8
                rows = tg * TG + ts * 128
                xl_ops[ts] = ir.op("act", lambda e, rows=rows, oc=oc, slot=slot: e.dma_start(
                    out=xres[slot][:], in_=x_q[rows:rows + 128, oc * 512:(oc + 1) * 512]),
                    deps=[xres_user.get(slot)], dma=True, chan=f"xr{slot}")
            for ts in range(TSG):
                bmm = pe_op(lambda e, ts=ts, oc=oc: e.matmul(
                    psc[ts][:], orb_sb[:], bo_bc[0:1, oc * 512:(oc + 1) * 512],
                    start=True, stop=False),
                    [d_orb, bank_prev.get(ts), atld[(ts // 4) * 4]] + bias_bc_done)
            for f4 in range(DC // 4):
                cst, wi = wstream4(wo, oc, f4)
                lastmm = None
                for fi in range(4):
                    f = f4 * 4 + fi
                    for ts in range(TSG):
                        deps = [cst, atld[(ts // 4) * 4 + f // (DC // 4)]]
                        acol = (ts // 4) * H * 512 + f * 512 + (ts % 4) * 128
                        mm = pe_op(lambda e, f=f, fi=fi, ts=ts, wi=wi, acol=acol: e.matmul(
                            psc[ts][:], attnT[:, acol:acol + 128],
                            w_bf2[wi % 2][:, fi * 512:(fi + 1) * 512],
                            start=False, stop=(f == DC - 1)), deps)
                        lastmm = mm
                        if f == DC - 1:
                            stop_ops[ts] = mm
                wpe_last[wi] = lastmm
            for ts in range(TSG):
                i = xi[0]; slot = i % 8
                rows = tg * TG + ts * 128
                ev = ir.op("dve", lambda e, ts=ts, slot=slot: e.tensor_tensor(
                    out=rst[slot][:], in0=psc[ts][:], in1=xres[slot][:], op=ALU.add),
                    deps=[stop_ops[ts], xl_ops[ts]] + (rst_user.get(slot) or []))
                bank_prev[ts] = ev
                xres_user[slot] = ev
                wr = ir.op("act", lambda e, rows=rows, oc=oc, slot=slot: e.dma_start(
                    out=r1_dram[rows:rows + 128, oc * 512:(oc + 1) * 512], in_=rst[slot][:]),
                    deps=[ev], dma=True, chan=f"rs{slot}")
                r1w[(oc, ts)] = wr
                rsp = ir.op("dve", lambda e, ts=ts, oc=oc, slot=slot: e.tensor_reduce(
                    out=spart[:, ts * NOC + oc:ts * NOC + oc + 1], in_=rst[slot][:],
                    axis=AX.X, op=ALU.add), deps=[ev, sp_reader.get(ts)])
                rqp = ir.op("act", lambda e, ts=ts, oc=oc, slot=slot: e.activation(
                    out=scr_bf[:, 0:512], in_=rst[slot][:], func=AF.Square,
                    accum_out=qpart[:, ts * NOC + oc:ts * NOC + oc + 1]),
                    deps=[ev, scr_waw[0]], )
                scr_waw[0] = rqp
                rst_user[slot] = [wr, rsp, rqp]
                if oc == NOC - 1:
                    pl_done[ts] = [rsp, rqp]
                xi[0] += 1
        r1_done = [(rst_user.get(s2) or [None])[0] for s2 in range(8)]
        freeall(pscg)
        g_attnT.__exit__(None, None, None)

        x1f, g_x1f = sb(f"x1f{tg}", [128, DC * TG], BF16)
        tp2g = []
        tp2 = []
        for i in range(2):
            t, g = ps(f"tp2{tg}_{i}", [128, 128], BF16); tp2g.append(g); tp2.append(t)
        x1row, g_x1row = sb(f"x1row{tg}", [128, D], BF16)
        x1f_ev = {}
        x1row_waw = [None]
        ln1_fin = ln_stats_all([d for v in pl_done.values() for d in v])
        ln1_done = {ts: ln1_fin for ts in range(TSG)}
        for ts in range(TSG):
            slot = ts % 2
            rows = tg * TG + ts * 128
            ld = ir.op("sp", lambda e, rows=rows, slot=slot: e.dma_start(
                out=rowbuf[slot][:], in_=r1_dram[rows:rows + 128, :]),
                deps=[r1w[(oc2, ts)] for oc2 in range(NOC)] + rowbuf_user.get(slot, []),
                dma=True, chan=f"row{slot}")
            fin = ln1_fin
            xb = ir.op("dve", lambda e, ts=ts, slot=slot: e.tensor_scalar(
                out=x1row[:], in0=rowbuf[slot][:], scalar1=stat_rstd[:, ts:ts + 1],
                scalar2=stat_mb[:, ts:ts + 1], op0=ALU.mult, op1=ALU.add),
                deps=fin + [ld, x1row_waw[0]])
            rowbuf_user[slot] = [xb]
            lasttr = None
            for Dc in range(DC):
                i2 = ts * DC + Dc
                tr = pe_op(lambda e, Dc=Dc, i2=i2: e.transpose(
                    tp2[i2 % 2][:], x1row[:, Dc * 128:(Dc + 1) * 128], id_sb[:]),
                    [xb, d_id, x1f_ev.get(i2 - 2)])
                evv = ir.op("dve", lambda e, Dc=Dc, ts=ts, i2=i2: e.tensor_copy(
                    out=x1f[:, Dc * TG + ts * 128:Dc * TG + (ts + 1) * 128], in_=tp2[i2 % 2][:]),
                    deps=[tr])
                x1f_ev[i2] = evv
                lasttr = tr
            x1row_waw[0] = lasttr
        x1f_done = x1f_ev[TSG * DC - 1]
        psh = []
        pshg = []
        for i in range(2):
            t, g = ps(f"psh{tg}_{i}", [128, 512]); pshg.append(g); psh.append(t)
        h_ev = {}
        for tq in range(TG // 512):
            stop = None
            last_ts = (tq + 1) * 4 - 1
            for Dc in range(DC):
                mm = pe_op(lambda e, Dc=Dc, tq=tq: e.matmul(
                    psh[tq % 2][:], at_bf[:, Dc * 128:(Dc + 1) * 128],
                    x1f[:, Dc * TG + tq * 512:Dc * TG + tq * 512 + 512],
                    start=(Dc == 0), stop=(Dc == DC - 1)),
                    [x1f_ev[last_ts * DC + DC - 1], atc, h_ev.get(tq - 2)])
                stop = mm
            hv = ir.op("act", lambda e, tq=tq: e.activation(
                out=hT_sb[:, tq * 512:tq * 512 + 512], in_=psh[tq % 2][:],
                func=AF.Identity, scale=coef[:]), deps=[stop, op_coef])
            h_ev[tq] = hv
        hT_done = [h_ev[tq] for tq in range(TG // 512)]
        freeall(pshg)
        freeall(tp2g)
        g_x1row.__exit__(None, None, None)

        pscg = []
        psc = []
        for i in range(TSG):
            t, g = ps(f"psd{tg}_{i}", [128, 512]); pscg.append(g); psc.append(t)
        for oc in range(NOC):
            stop_ops = {}
            xl_ops = {}
            for ts in range(TSG):
                i = xi[0] + ts; slot = i % 8
                rows = tg * TG + ts * 128
                rl = ir.op("sp", lambda e, rows=rows, oc=oc, slot=slot: e.dma_start(
                    out=xres[slot][:], in_=r1_dram[rows:rows + 128, oc * 512:(oc + 1) * 512]),
                    deps=[r1w[(oc, ts)], xres_user.get(slot)], dma=True, chan=f"xr{slot}")
                xc = ir.op("act", lambda e, ts=ts, slot=slot: e.activation(
                    out=xres[slot][:], in_=xres[slot][:], func=AF.Identity,
                    scale=stat_rstd[:, ts:ts + 1], bias=stat_mb[:, ts:ts + 1]),
                    deps=[rl] + ln1_done[ts])
                c2_xc_last[0] = xc
                xl_ops[ts] = xc
            for ts in range(TSG):
                bmm = pe_op(lambda e, ts=ts, oc=oc: e.matmul(
                    psc[ts][:], orb_sb[:], bff_bc[0:1, oc * 512:(oc + 1) * 512],
                    start=True, stop=False),
                    [d_orb, bank_prev.get(ts)] + bias_bc_done + (hT_done if oc == 0 else []))
            for f4 in range(DC // 4):
                cst, wi = wstream4(wff, oc, f4)
                lastmm = None
                for fi in range(4):
                    f = f4 * 4 + fi
                    for ts in range(TSG):
                        deps = [cst, x1f_ev[ts * DC + f]]
                        mm = pe_op(lambda e, f=f, fi=fi, ts=ts, wi=wi: e.matmul(
                            psc[ts][:], x1f[:, f * TG + ts * 128:f * TG + (ts + 1) * 128],
                            w_bf2[wi % 2][:, fi * 512:(fi + 1) * 512],
                            start=False, stop=False), deps)
                        lastmm = mm
                wpe_last[wi] = lastmm
            for ts in range(TSG):
                lmm = pe_op(lambda e, ts=ts, oc=oc: e.matmul(
                    psc[ts][:], hT_sb[:, ts * 128:(ts + 1) * 128],
                    bt_bf[:, oc * 512:(oc + 1) * 512], start=False, stop=True),
                    hT_done + [btc])
                stop_ops[ts] = lmm
                last_ffn_mm[0] = lmm
            for ts in range(TSG):
                i = xi[0]; slot = i % 8
                rows = tg * TG + ts * 128
                ev = ir.op("dve", lambda e, ts=ts, slot=slot: e.tensor_tensor(
                    out=rst[slot][:], in0=psc[ts][:], in1=xres[slot][:], op=ALU.add),
                    deps=[stop_ops[ts], xl_ops[ts]] + (rst_user.get(slot) or []))
                bank_prev[ts] = ev
                xres_user[slot] = ev
                wr = ir.op("act", lambda e, rows=rows, oc=oc, slot=slot: e.dma_start(
                    out=r2_dram[rows:rows + 128, oc * 512:(oc + 1) * 512], in_=rst[slot][:]),
                    deps=[ev], dma=True, chan=f"rs{slot}")
                r2w[(oc, ts)] = wr
                rsp = ir.op("dve", lambda e, ts=ts, oc=oc, slot=slot: e.tensor_reduce(
                    out=spart[:, ts * NOC + oc:ts * NOC + oc + 1], in_=rst[slot][:],
                    axis=AX.X, op=ALU.add), deps=[ev, sp_reader.get(ts)])
                rqp = ir.op("act", lambda e, ts=ts, oc=oc, slot=slot: e.activation(
                    out=scr_bf[:, 0:512], in_=rst[slot][:], func=AF.Square,
                    accum_out=qpart[:, ts * NOC + oc:ts * NOC + oc + 1]),
                    deps=[ev, scr_waw[0]], )
                scr_waw[0] = rqp
                rst_user[slot] = [wr, rsp, rqp]
                if oc == NOC - 1:
                    pl_done[ts] = [rsp, rqp]
                xi[0] += 1
        r2_done = [(rst_user.get(s2) or [None])[0] for s2 in range(8)]
        freeall(pscg)
        g_x1f.__exit__(None, None, None)

        ln2_fin = ln_stats_all([d for v in pl_done.values() for d in v] + [c2_xc_last[0]])
        D2 = D // 2
        for ts in range(TSG):
            slot = ts % 2
            rows = tg * TG + ts * 128
            lds = []
            ldeps = [r2w[(oc2, ts)] for oc2 in range(NOC)]
            ldeng = "sp" if tg == NTG - 1 else "act"
            for hv in range(2):
                lds.append(ir.op(ldeng, lambda e, rows=rows, slot=slot, hv=hv: e.dma_start(
                    out=rowbuf[slot][:, hv * D2:(hv + 1) * D2],
                    in_=r2_dram[rows:rows + 128, hv * D2:(hv + 1) * D2]),
                    deps=ldeps + [c2_xc_last[0]] + rowbuf_user.get(slot, []),
                    dma=True, chan=f"row{slot}" if hv == 0 else f"row{slot}b"))
            fin = ln2_fin
            oa2 = []
            D4 = D // 4
            for qv in range(4):
                osl = (ts * 4 + qv) % 5
                oa = ir.op("dve", lambda e, ts=ts, slot=slot, qv=qv, osl=osl, D4=D4: e.tensor_scalar(
                    out=outst[osl][:], in0=rowbuf[slot][:, qv * D4:(qv + 1) * D4],
                    scalar1=stat_rstd[:, ts:ts + 1],
                    scalar2=stat_mb[:, ts:ts + 1], op0=ALU.mult, op1=ALU.add),
                    deps=fin + [lds[qv // 2]] + (outlast.get(osl) or []))
                oa2.append(oa)
                ow = ir.op("act", lambda e, rows=rows, qv=qv, osl=osl, D4=D4: e.dma_start(
                    out=y[rows:rows + 128, qv * D4:(qv + 1) * D4],
                    in_=outst[osl][:]),
                    deps=[oa], dma=True, chan=f"out{osl}")
                outlast[osl] = [ow]
                out_wr.append(ow)
            rowbuf_user[slot] = oa2 + fin

    ir.op("sp", None, deps=out_wr[-5:])

    ir.resolve(nc, stack)
    ir.emit(nc)
    stack.close()
    freeall(gc)
    freeall(guards)
    nc.finalize()
    return nc



def make_core_inputs(cfg, x, wq, bq, wk, bk, wv, bv, wo, bo, la, lb, wff, bff, lwts, b, half):
    T, D = cfg["T"], cfg["D"]
    blocks = half_blocks(T, half)
    idx = np.concatenate([np.arange(bb * 128, (bb + 1) * 128) for bb in blocks])
    xb = np.asarray(x[b], np.float32)
    xt = np.ascontiguousarray(xb.T)
    NR = int(np.asarray(la).shape[0]) * int(np.asarray(la).shape[1])
    d = dict(
        xt_kv=xt,
        xt_q=np.ascontiguousarray(xt[:, idx]),
        x_q=np.ascontiguousarray(xb[idx]),
        wq=np.asarray(wq, np.float32), wk=np.asarray(wk, np.float32),
        wv=np.asarray(wv, np.float32), wo=np.asarray(wo, np.float32),
        wff=np.ascontiguousarray(np.asarray(wff, np.float32).T),
        at=np.ascontiguousarray(np.asarray(la, np.float32).transpose(2, 0, 1).reshape(D, NR)),
        bt=np.ascontiguousarray(np.asarray(lb, np.float32).transpose(0, 2, 1).reshape(NR, D)),
        bq=np.asarray(bq, np.float32), bk=np.asarray(bk, np.float32),
        bv=np.asarray(bv, np.float32), bo=np.asarray(bo, np.float32),
        bff=np.asarray(bff, np.float32),
        lw=np.repeat(np.asarray(lwts, np.float32), int(np.asarray(la).shape[1])),
        masks=build_masks(cfg, half),
        ident=np.eye(128, dtype=BF16NP),
        onescol=np.ones((128, 1), BF16NP),
        onesrowf=np.ones((1, 128), np.float32),
        onesrowb=np.ones((1, 128), BF16NP),
    )
    return d, idx


def kernel(L0_attn_x, L0_attn_Wq, L0_attn_bq, L0_attn_Wk, L0_attn_bk,
           L0_attn_Wv, L0_attn_bv, L0_attn_Wo, L0_attn_bo,
           L0_ffn_lora_A, L0_ffn_lora_B, L0_ffn_W, L0_ffn_bias, L0_ffn_lora_weights):
    cfg = real_cfg()
    B = 4
    nc = build_program(cfg)
    in_maps = []
    idxs = []
    for c in range(8):
        b, half = c // 2, c % 2
        d, idx = make_core_inputs(cfg, L0_attn_x, L0_attn_Wq, L0_attn_bq,
                                  L0_attn_Wk, L0_attn_bk, L0_attn_Wv, L0_attn_bv,
                                  L0_attn_Wo, L0_attn_bo, L0_ffn_lora_A, L0_ffn_lora_B,
                                  L0_ffn_W, L0_ffn_bias, L0_ffn_lora_weights, b, half)
        in_maps.append(d)
        idxs.append((b, idx))
    res = run_bass_kernel_spmd(nc, in_maps, core_ids=list(range(8)))
    out = np.empty((B, cfg["T"], cfg["D"]), np.float32)
    for c in range(8):
        b, idx = idxs[c]
        out[b, idx] = np.asarray(res.results[c]["y"], np.float32)
    return out
